# version 16
# baseline (speedup 1.0000x reference)
import numpy as np
import concourse.bass as bass
import concourse.mybir as mybir
from concourse.bass_utils import run_bass_kernel_spmd

F32 = mybir.dt.float32
BF16 = mybir.dt.bfloat16
I32 = mybir.dt.int32
AF = mybir.ActivationFunctionType
ALU = mybir.AluOpType
AX = mybir.AxisListType

B, S, D = 2, 4096, 4096
NH, NKV, HD = 32, 8, 128
SCALE = HD ** -0.5
TOK = 1024
NROW = 8
DC = D // 128
E, R = 8, 16
NEXP = 15
EPS = 1e-5

N_SUB = 3

T0 = ("sT", 0)
V0 = ("sV", 0)
A0 = ("sA", 0)
O0 = ("sO", 0)


def rows_for_rank(r):
    return sorted({r, 7 - r, 8 + r, 15 - r, 16 + r, 23 - r, 24 + r, 31 - r})


class P:

    def __init__(self, nc, env):
        self.nc = nc
        self.env = env
        self.q = {k: [] for k in ("sync", "act", "gpsimd", "tensor", "vector")}
        self.cnt = {}
        self.tok = {}

    def op(self, eng, emit, waits=(), sem=None, by=1):
        if sem is None:
            sem = {"act": "sA", "tensor": "sT", "vector": "sV"}[eng]
        s = self.env[sem]
        self.cnt[sem] = self.cnt.get(sem, 0) + by
        tgt = self.cnt[sem]
        ws = tuple((self.env[ks], v) for (ks, v) in waits if v > 0)

        def f(e, emit=emit, ws=ws, s=s, by=by):
            for (sm, v) in ws:
                e.wait_ge(sm, v)
            r = emit(e)
            if r is not None and by > 0:
                r.then_inc(s, by)
        self.q[eng].append(f)
        return (sem, tgt)

    def multi(self, eng, emit, waits=(), sem=None, n=1, by=16):
        s = self.env[sem]
        ws = tuple((self.env[ks], v) for (ks, v) in waits if v > 0)

        def f(e, emit=emit, ws=ws, s=s):
            for (sm, v) in ws:
                e.wait_ge(sm, v)
            emit(e, s)
        self.q[eng].append(f)
        self.cnt[sem] = self.cnt.get(sem, 0) + n * by
        return (sem, self.cnt[sem])

    def plan(self):
        env = self.env
        self.prologue()
        self.tok["x_src"] = []
        self.tok["x_src_chunks"] = {}
        self.ffn_sublayer(src=env["x_in"], dst=env["x_res"] if N_SUB > 1 else env["out_x"],
                          next_src=env["x_res"] if N_SUB > 1 else None)
        if N_SUB >= 2:
            self.attn_sublayer(dst=env["x_res"] if N_SUB > 2 else env["out_x"],
                               next_src=env["x_res"] if N_SUB > 2 else None)
        if N_SUB >= 3:
            self.ffn_sublayer(src=env["x_res"], dst=env["out_x"])
        self.op("act", lambda a: a.nop(), waits=list(self.tok["last_store"]), by=0)

    def prologue(self):
        env = self.env
        t_ones = self.op("vector", lambda v: v.memset(env["ones_bf"][:, :], 1.0))
        t_ones = self.op("vector", lambda v: v.memset(env["ones_row"][:, :], 1.0))
        t_ones = self.op("vector", lambda v: v.memset(env["ones_rbf"][:, :], 1.0))
        t_ones = self.op("vector", lambda v: v.memset(env["qn2max"][:, :], 0.0))
        t_ones = self.op("vector", lambda v: v.memset(env["kn2max"][:, :], 0.0))
        self.tok["ones"] = t_ones
        self.tok["norminit"] = t_ones

        def consts(sy, s):
            sy.dma_start(env["ident_sb"][:, :], env["ident32"][:, :]).then_inc(s, 16)
            sy.dma_start(env["identb_sb"][:, :], env["identb"][:, :]).then_inc(s, 16)
            sy.dma_start(env["rbias_sb"][:, :], env["rb_s"][:, :]).then_inc(s, 16)
            sy.dma_start(env["iota_sb"][:, :], env["iota8"][:, :]).then_inc(s, 16)
            sy.dma_start(env["halfb"][:, :], env["bb_s"][:, :]).then_inc(s, 16)
            sy.dma_start(env["e16_sb"][:, :], env["e16"][:, :]).then_inc(s, 16)
            sy.dma_start(env["emap_i"][:, :], env["emap"][:, :]).then_inc(s, 16)
        t_consts = self.multi("sync", consts, n=7, sem="sCN")
        self.tok["consts"] = t_consts

        def gp(g, s):
            g.dma_start(env["rw_sb"][:, :, :], env["rw_s"][:, :, :]).then_inc(s, 16)
            g.dma_start(env["a_sb"][:, :, :], env["a_s"][:, :, :]).then_inc(s, 16)
            g.dma_start(env["b_sb"][:, :, :], env["b2_s"][:, :, :]).then_inc(s, 16)
        t_gp = self.multi("gpsimd", gp, n=3, sem="sSW")
        self.tok["small_w"] = t_gp

        t = self.op("vector", lambda v: v.tensor_copy(env["emap_f"][:, :], env["emap_i"][:, :]),
                    waits=[t_consts])
        t_oneh = self.op("vector", lambda v: v.tensor_scalar(
            out=env["m_onehot"][:, :], in0=env["iota_sb"][:, :],
            scalar1=env["emap_f"][:, :], scalar2=0.0, op0=ALU.is_equal, op1=ALU.add),
            waits=[t])
        t_hb = self.op("vector", lambda v: v.tensor_scalar(
            out=env["halfb"][:, :], in0=env["halfb"][:, :],
            scalar1=0.5, scalar2=0.0, op0=ALU.mult, op1=ALU.add),
            waits=[t_consts])
        self.tok["pro_v"] = t_hb
        self.tok["oneh"] = t_oneh
        for k in ("xn_free", "pA_free_T", "wslab_free"):
            self.tok[k] = T0
        for k in ("pA_free", "pB_free", "pC_free", "pD_free",
                  "resch_free", "tmpf_free"):
            self.tok[k] = V0
        self.tok["x1b_free"] = {0: V0, 1: V0}

    def emit_stats_load(self, src, dc, store_toks):
        env = self.env
        xch, xbf, x2bf = env["xchunk"], env["xbf"], env["x2bf"]
        bf = dc % 2
        xch_user = self.tok["xch_user"]
        xbf_user = self.tok["xbf_user"]
        pre = self.tok.setdefault("pre_stats", {})
        sq = self.tok.setdefault("pre_sq", {})
        t_ld = self.op("sync", lambda sy, dc=dc, bf=bf: sy.dma_start(
            xch[:, bf, :], src[dc, :, :]),
            waits=[xch_user[bf], store_toks.get((dc, 0), V0), store_toks.get((dc, 1), V0)],
            by=16, sem="sXa" if bf == 0 else "sXb")
        t_c1 = self.op("vector", lambda v, bf=bf: v.tensor_copy(xbf[:, bf, :], xch[:, bf, :]),
                       waits=[t_ld, self.tok["ones"], xbf_user[bf]])
        t_c2 = self.op("act", lambda a, bf=bf: a.activation(
            out=x2bf[:, bf, :], in_=xbf[:, bf, :], func=AF.Square),
            waits=[t_c1, pre.get(dc - 2, self.tok.get("x2bf_free", T0))])
        sq[dc] = t_c2
        xch_user[bf] = t_c1

    def emit_stats_mm(self, dc):
        env = self.env
        xbf, x2bf = env["xbf"], env["x2bf"]
        pS = env["pS"]
        bf = dc % 2
        pre = self.tok["pre_stats"]

        def mm(t, dc=dc, bf=bf):
            st, sp = dc == 0, dc == DC - 1
            t.matmul(pS[0:1, 2, :], env["ones_bf"][:, :], xbf[:, bf, 0:512], start=st, stop=sp, skip_group_check=True)
            t.matmul(pS[0:1, 3, :], env["ones_bf"][:, :], xbf[:, bf, 512:1024], start=st, stop=sp, skip_group_check=True)
            t.matmul(pS[32:33, 2, :], env["ones_bf"][:, :], x2bf[:, bf, 0:512], start=st, stop=sp, skip_group_check=True)
            return t.matmul(pS[32:33, 3, :], env["ones_bf"][:, :], x2bf[:, bf, 512:1024], start=st, stop=sp, skip_group_check=True)
        w = [self.tok["pre_sq"][dc]]
        if dc == 0:
            w.append(self.tok["pB_free"])
        pre[dc] = self.op("tensor", mm, waits=w)
        self.tok["xbf_user"][bf] = pre[dc]

    def ln_pass(self, src):
        env = self.env
        xch, xbf, x2bf = env["xchunk"], env["xbf"], env["x2bf"]
        pS, pC = env["pS"], env["pC"]
        rows, rows2, rows3, rows4 = env["rows"], env["rows2"], env["rows3"], env["rows4"]
        xch_user = self.tok.setdefault("xch_user", {0: V0, 1: V0})
        xbf_user = self.tok.setdefault("xbf_user", {0: T0, 1: T0})

        mm_done = self.tok.pop("pre_stats", None)
        self.tok.pop("pre_sq", None)
        for dc in (() if mm_done is not None else range(DC)):
            bf = dc % 2
            ck = self.tok["x_src_chunks"]
            t_ld = self.op("sync", lambda sy, dc=dc, bf=bf: sy.dma_start(
                xch[:, bf, :], src[dc, :, :]),
                waits=[xch_user[bf], ck.get((dc, 0), V0), ck.get((dc, 1), V0)], by=16,
                sem="sXa" if bf == 0 else "sXb")
            t_c1 = self.op("vector", lambda v, bf=bf: v.tensor_copy(xbf[:, bf, :], xch[:, bf, :]),
                           waits=[t_ld, self.tok["ones"], xbf_user[bf]])
            t_c2 = self.op("act", lambda a, bf=bf: a.activation(
                out=x2bf[:, bf, :], in_=xbf[:, bf, :], func=AF.Square),
                waits=[t_c1, (mm_done or {}).get(dc - 2, self.tok.get("x2bf_free", T0))])
            cast_done = t_c2
            xch_user[bf] = t_c1

            def mm(t, dc=dc, bf=bf):
                st, sp = dc == 0, dc == DC - 1
                t.matmul(pS[0:1, 2, :], env["ones_bf"][:, :], xbf[:, bf, 0:512], start=st, stop=sp)
                t.matmul(pS[0:1, 3, :], env["ones_bf"][:, :], xbf[:, bf, 512:1024], start=st, stop=sp)
                t.matmul(pS[32:33, 2, :], env["ones_bf"][:, :], x2bf[:, bf, 0:512], start=st, stop=sp)
                return t.matmul(pS[32:33, 3, :], env["ones_bf"][:, :], x2bf[:, bf, 512:1024], start=st, stop=sp)
            w = [cast_done]
            if dc == 0:
                w += [self.tok["pB_free"]]
            if mm_done is None:
                mm_done = {}
            mm_done[dc] = self.op("tensor", mm, waits=w)
            xbf_user[bf] = mm_done[dc]

        c = 1.0 / D

        def r1(v):
            v.tensor_scalar(out=rows[:, 0:512], in0=pS[0:1, 2, :], scalar1=c, scalar2=0.0, op0=ALU.mult, op1=ALU.add)
            v.tensor_scalar(out=rows[:, 512:1024], in0=pS[0:1, 3, :], scalar1=c, scalar2=0.0, op0=ALU.mult, op1=ALU.add)
            v.tensor_scalar(out=rows2[:, 0:512], in0=pS[32:33, 2, :], scalar1=c, scalar2=0.0, op0=ALU.mult, op1=ALU.add)
            return v.tensor_scalar(out=rows2[:, 512:1024], in0=pS[32:33, 3, :], scalar1=c, scalar2=0.0, op0=ALU.mult, op1=ALU.add)
        t = self.op("vector", r1, waits=[mm_done[DC - 1]])
        self.tok["pB_free"] = t

        t = self.op("vector", lambda v: v.tensor_tensor(out=rows3[:, :], in0=rows[:, :], in1=rows[:, :], op=ALU.mult), waits=[t])
        t = self.op("vector", lambda v: v.tensor_tensor(out=rows2[:, :], in0=rows2[:, :], in1=rows3[:, :], op=ALU.subtract), waits=[t])
        t = self.op("vector", lambda v: v.tensor_scalar(out=rows2[:, :], in0=rows2[:, :], scalar1=EPS, scalar2=0.0, op0=ALU.add, op1=ALU.add), waits=[t])
        t = self.op("act", lambda a: a.activation(out=rows3[:, :], in_=rows2[:, :], func=AF.Sqrt), waits=[t])
        t = self.op("vector", lambda v: v.reciprocal(out=rows3[:, :], in_=rows3[:, :]), waits=[t])
        t = self.op("vector", lambda v: v.tensor_tensor(out=rows4, in0=rows3[:, :], in1=rows3[:, :], op=ALU.mult), waits=[t])
        t = self.op("vector", lambda v: v.tensor_tensor(out=rows4, in0=rows4, in1=rows2[:, :], op=ALU.mult), waits=[t])
        t = self.op("vector", lambda v: v.tensor_scalar(out=rows4, in0=rows4, scalar1=-0.5, scalar2=1.5, op0=ALU.mult, op1=ALU.add), waits=[t])
        t_rstd = self.op("vector", lambda v: v.tensor_tensor(out=rows3[:, :], in0=rows3[:, :], in1=rows4[:, :], op=ALU.mult), waits=[t])

        xn = env["xn"]
        norm_done = {}
        tmpf_user = {0: self.tok["tmpf_free"], 1: self.tok["tmpf_free"]}
        prev_half = None
        for h in (0, 1):
            def bc(t_, h=h):
                t_.matmul(pC[:, 0, :], env["ones_row"][:, :], rows[:, h * 512:(h + 1) * 512], start=True, stop=True)
                return t_.matmul(pC[:, 1, :], env["ones_row"][:, :], rows3[:, h * 512:(h + 1) * 512], start=True, stop=True)
            w = [t_rstd, self.tok["pC_free"]]
            if prev_half is not None:
                w.append(prev_half)
            t_bc = self.op("tensor", bc, waits=w)

            for dc in range(DC):
                bf = dc % 2
                t_ld = self.op("sync", lambda sy, dc=dc, bf=bf, h=h: sy.dma_start(
                    xch[:, bf, h * 512:(h + 1) * 512], src[dc, :, h * 512:(h + 1) * 512]),
                    waits=[xch_user[bf], self.tok["x_src_chunks"].get((dc, h), V0)],
                    by=16, sem="sXa" if bf == 0 else "sXb")
                t_n1 = self.op("vector", lambda v, bf=bf, h=h: v.tensor_tensor(
                    out=env["tmpf"][:, bf, :], in0=xch[:, bf, h * 512:(h + 1) * 512], in1=pC[:, 0, :], op=ALU.subtract),
                    waits=[t_ld, t_bc, tmpf_user[bf]])
                t_n2 = self.op("vector", lambda v, dc=dc, bf=bf, h=h: v.tensor_tensor(
                    out=xn[:, dc, h * 512:(h + 1) * 512], in0=env["tmpf"][:, bf, :], in1=pC[:, 1, :], op=ALU.mult),
                    waits=[t_n1, self.tok["xn_free"] if (h == 0 and dc == 0) else T0])
                norm_done[(h, dc)] = t_n2
                xch_user[bf] = t_n1
                tmpf_user[bf] = t_n2
            prev_half = norm_done[(h, DC - 1)]
        self.tok["pC_free"] = norm_done[(1, DC - 1)]
        self.tok["tmpf_free"] = norm_done[(1, DC - 1)]
        self.tok["x2bf_free"] = mm_done[DC - 1]
        return (norm_done[(0, DC - 1)], norm_done[(1, DC - 1)])

    def ffn_sublayer(self, src, dst, next_src=None):
        env = self.env
        xn = env["xn"]
        pS, pC = env["pS"], env["pC"]

        xn_h = self.ln_pass(src)
        xn_ok = xn_h[1]

        def rmm(t, h):
            r = None
            for dc in range(DC):
                r = t.matmul(pS[0:NEXP, 2 + h, :], env["rw_sb"][:, dc, :],
                             xn[:, dc, h * 512:(h + 1) * 512], start=(dc == 0), stop=(dc == DC - 1))
            return r
        t_rmm = self.op("tensor", lambda t: rmm(t, 0),
                        waits=[xn_h[0], self.tok["small_w"], self.tok["pB_free"]])
        t_rmm = self.op("tensor", lambda t: rmm(t, 1), waits=[xn_h[1]])

        def revac(v):
            v.tensor_scalar(out=env["logit_sb"][:, 0:512], in0=pS[0:NEXP, 2, :],
                            scalar1=env["rbias_sb"][:, :], scalar2=1.0, op0=ALU.add, op1=ALU.mult)
            return v.tensor_scalar(out=env["logit_sb"][:, 512:1024], in0=pS[0:NEXP, 3, :],
                                   scalar1=env["rbias_sb"][:, :], scalar2=1.0, op0=ALU.add, op1=ALU.mult)
        t_lg = self.op("vector", revac, waits=[t_rmm, self.tok["pro_v"], self.tok["oneh"]])
        self.tok["pB_free"] = t_lg

        def ltr(t):
            r = None
            for rt in range(NROW):
                r = t.transpose(pC[:, 0, rt * 64:rt * 64 + NEXP],
                                env["logit_sb"][0:NEXP, rt * 128:(rt + 1) * 128],
                                env["ident_sb"][0:NEXP, 0:NEXP])
            return r
        t_ltr = self.op("tensor", ltr, waits=[t_lg, self.tok["pC_free"], self.tok["consts"]])

        probs, racc, spw = env["probs"], env["racc"], env["spw"]
        rmax, rsum = env["rmax"], env["rsum"]

        def smax1(v):
            r = None
            for rt in range(NROW):
                r = v.tensor_scalar(out=probs[:, rt, :], in0=pC[:, 0, rt * 64:rt * 64 + NEXP],
                                    scalar1=1.0, scalar2=-3.0e38, op0=ALU.mult, op1=ALU.max,
                                    accum_out=rmax[:, rt:rt + 1])
            return r
        t = self.op("vector", smax1, waits=[t_ltr])
        self.tok["pC_free"] = t
        t = self.op("vector", lambda v: v.tensor_scalar(
            out=rmax[:, :], in0=rmax[:, :], scalar1=-1.0, scalar2=0.0, op0=ALU.mult, op1=ALU.add), waits=[t])

        def smax3(a):
            r = None
            for rt in range(NROW):
                r = a.activation(out=probs[:, rt, :], in_=probs[:, rt, :], func=AF.Exp,
                                 bias=rmax[:, rt:rt + 1], accum_out=rsum[:, rt:rt + 1])
            return r
        t = self.op("act", smax3, waits=[t])
        t = self.op("vector", lambda v: v.reciprocal(out=rsum[:, :], in_=rsum[:, :]), waits=[t])

        def smax5(v):
            for rt in range(NROW):
                v.tensor_scalar(out=probs[:, rt, :], in0=probs[:, rt, :],
                                scalar1=rsum[:, rt:rt + 1], scalar2=0.0, op0=ALU.mult, op1=ALU.add)
            v.memset(racc[:, :, :], 0.0)
            return v.memset(env["racc2"][:, :, :], 0.0)
        t = self.op("vector", smax5, waits=[t])

        ta = tb = t
        for m in range(NEXP):
            acc = racc if m % 2 == 0 else env["racc2"]

            def cnt(v, m=m, acc=acc):
                r = None
                for rt in range(NROW):
                    r = v.scalar_tensor_tensor(out=acc[:, rt, :], in0=probs[:, rt, :],
                                               scalar=probs[:, rt, m:m + 1], in1=acc[:, rt, :],
                                               op0=ALU.is_lt, op1=ALU.add)
                return r
            if m % 2 == 0:
                ta = self.op("vector", cnt, waits=[ta])
            else:
                tb = self.op("vector", cnt, waits=[tb])

        def cmerge(v):
            r = None
            for rt in range(NROW):
                r = v.tensor_tensor(out=racc[:, rt, :], in0=racc[:, rt, :],
                                    in1=env["racc2"][:, rt, :], op=ALU.add)
            return r
        t = self.op("vector", cmerge, waits=[ta, tb])

        def selw(v):
            r = None
            for rt in range(NROW):
                r = v.scalar_tensor_tensor(out=spw[:, rt, :], in0=racc[:, rt, :],
                                           scalar=8.0, in1=probs[:, rt, :],
                                           op0=ALU.is_lt, op1=ALU.mult)
            return r
        t_spw = self.op("vector", selw, waits=[t])

        def str_(t):
            r = None
            for rt in range(NROW):
                r = t.transpose(pC[0:NEXP, rt // 4, (rt % 4) * 128:(rt % 4 + 1) * 128],
                                spw[:, rt, :], env["ident_sb"][:, :])
            return r
        t_str = self.op("tensor", str_, waits=[t_spw, self.tok["pC_free"]])

        def spwt(v):
            v.tensor_copy(env["spw_t"][:, 0:512], pC[0:NEXP, 0, :])
            return v.tensor_copy(env["spw_t"][:, 512:1024], pC[0:NEXP, 1, :])
        t_spwt = self.op("vector", spwt, waits=[t_str])
        self.tok["pC_free"] = t_spwt

        def wmm(t):
            t.matmul(pS[0:E, 2, :], env["m_onehot"][:, :], env["spw_t"][:, 0:512], start=True, stop=True)
            return t.matmul(pS[0:E, 3, :], env["m_onehot"][:, :], env["spw_t"][:, 512:1024], start=True, stop=True)
        t_wmm = self.op("tensor", wmm, waits=[t_spwt, self.tok["pB_free"]])

        def wevac(v):
            v.tensor_copy(env["wt_sb"][:, 0:512], pS[0:E, 2, :])
            return v.tensor_copy(env["wt_sb"][:, 512:1024], pS[0:E, 3, :])
        t_wt = self.op("vector", wevac, waits=[t_wmm])
        self.tok["pB_free"] = t_wt

        def wrepmm(t):
            t.matmul(pS[:, 2, :], env["e16_sb"][:, :], env["wt_sb"][:, 0:512], start=True, stop=True)
            return t.matmul(pS[:, 3, :], env["e16_sb"][:, :], env["wt_sb"][:, 512:1024], start=True, stop=True)
        t_wrep = self.op("tensor", wrepmm, waits=[t_wt])

        def wrepev(v):
            v.tensor_copy(env["wrep_sb"][:, 0:512], pS[:, 2, :])
            return v.tensor_copy(env["wrep_sb"][:, 512:1024], pS[:, 3, :])
        t_wrepev = self.op("vector", wrepev, waits=[t_wrep])
        self.tok["pB_free"] = t_wrepev

        def hmm(t):
            r = None
            for h in (0, 1):
                for dc in range(DC):
                    r = t.matmul(pC[:, h, :], env["a_sb"][:, dc, :], xn[:, dc, h * 512:(h + 1) * 512],
                                 start=(dc == 0), stop=(dc == DC - 1))
            return r
        t_hmm = self.op("tensor", hmm, waits=[xn_ok, self.tok["pC_free"]])
        del xn_ok

        def hwmul(v):
            v.tensor_tensor(out=env["hw_sb"][:, 0:512], in0=pC[:, 0, :], in1=env["wrep_sb"][:, 0:512], op=ALU.mult)
            return v.tensor_tensor(out=env["hw_sb"][:, 512:1024], in0=pC[:, 1, :], in1=env["wrep_sb"][:, 512:1024], op=ALU.mult)
        t_hw = self.op("vector", hwmul, waits=[t_hmm, t_wrepev])
        self.tok["pC_free"] = t_hw

        wslab, x1b, tmpf, resch = env["wslab"], env["x1buf"], env["tmpf"], env["resch"]
        slab_mm = {}
        evd = {}
        std = {}
        for oc in range(DC):
            wb = oc % 2
            t_w = self.op("gpsimd", lambda g, oc=oc, wb=wb: g.dma_start(
                wslab[:, wb, :, :], env["bw_s"][oc, :, :, :]),
                waits=[slab_mm.get(oc - 2, self.tok["wslab_free"])], by=16,
                sem="sWa" if wb == 0 else "sWb")

            for h in (0, 1):
                ob = h

                def mms(t, oc=oc, h=h, wb=wb, ob=ob):
                    for dc in range(DC):
                        t.matmul(pS[:, ob, :], wslab[:, wb, dc, :], xn[:, dc, h * 512:(h + 1) * 512],
                                 start=(dc == 0), stop=False)
                    return t.matmul(pS[:, ob, :], env["b_sb"][:, oc, :], env["hw_sb"][:, h * 512:(h + 1) * 512],
                                    start=False, stop=True)
                w = [t_w, evd.get((oc - 1, h), self.tok["pA_free"])]
                if oc == 0:
                    w += [xn_h[h], t_hw, self.tok["small_w"]]
                t_mm = self.op("tensor", mms, waits=w)
                slab_mm[oc] = t_mm

                t_r = self.op("sync", lambda sy, oc=oc, h=h, ob=ob: sy.dma_start(
                    resch[:, ob, :], src[oc, :, h * 512:(h + 1) * 512]),
                    waits=[evd.get((oc - 1, h), self.tok["resch_free"]),
                           self.tok["x_src_chunks"].get((oc, h), V0)],
                    by=16, sem="sRa" if ob == 0 else "sRb")

                t_e1 = self.op("vector", lambda v, oc=oc, ob=ob: v.tensor_scalar(
                    out=tmpf[:, ob, :], in0=pS[:, ob, :],
                    scalar1=0.5, scalar2=env["halfb"][:, oc:oc + 1], op0=ALU.mult, op1=ALU.add),
                    waits=[t_mm, self.tok["pro_v"],
                           evd.get((oc - 1, h), self.tok["tmpf_free"])])
                t_e2 = self.op("vector", lambda v, ob=ob: v.tensor_tensor(
                    out=x1b[:, ob, :], in0=tmpf[:, ob, :], in1=resch[:, ob, :], op=ALU.add),
                    waits=[t_e1, t_r, std.get((oc - 1, h), self.tok["x1b_free"][h])])
                evd[(oc, h)] = t_e2

                t_s = self.op("act", lambda sc, oc=oc, h=h, ob=ob: sc.dma_start(
                    dst[oc, :, h * 512:(h + 1) * 512], x1b[:, ob, :]),
                    waits=[t_e2], by=16, sem="sOa" if ob == 0 else "sOb")
                std[(oc, h)] = t_s
            if next_src is not None:
                if oc >= 4:
                    self.emit_stats_mm(oc - 4)
                if oc >= 2:
                    self.emit_stats_load(next_src, oc - 2, std)
        if next_src is not None:
            for dcx in (DC - 2, DC - 1):
                self.emit_stats_mm(dcx - 2)
                self.emit_stats_load(next_src, dcx, std)
            for dcx in (DC - 2, DC - 1):
                self.emit_stats_mm(dcx)

        self.tok["pA_free"] = evd[(DC - 1, 1)]
        self.tok["xn_free"] = slab_mm[DC - 1]
        self.tok["wslab_free"] = slab_mm[DC - 1]
        self.tok["resch_free"] = evd[(DC - 1, 1)]
        self.tok["tmpf_free"] = evd[(DC - 1, 1)]
        self.tok["x1b_free"] = {0: std[(DC - 1, 0)], 1: std[(DC - 1, 1)]}
        self.tok["last_store"] = [std[(DC - 1, 0)], std[(DC - 1, 1)]]
        self.tok["x_src"] = [std[(DC - 1, 0)], std[(DC - 1, 1)]]
        self.tok["x_src_chunks"] = dict(std)

    def attn_sublayer(self, dst, next_src=None):
        env = self.env
        xn = env["xn"]
        pS, pC = env["pS"], env["pC"]
        wslab = env["wslab"]
        x_res = env["x_res"]

        xn_h = self.ln_pass(x_res)

        kv_store = {}
        slab_mm = {}
        evp = {}
        vtst_free = O0
        vtr_done = {}
        for idx in range(2 * NKV):
            isv = idx >= NKV
            g = idx % NKV
            wb = idx % 2
            wsrc = env["wv_s"] if isv else env["wk_s"]
            t_w = self.op("gpsimd", lambda gp, g=g, wb=wb, wsrc=wsrc: gp.dma_start(
                wslab[:, wb, :, :], wsrc[g, :, :, :]),
                waits=[slab_mm.get(idx - 2, self.tok["wslab_free"])], by=16,
                sem="sWa" if wb == 0 else "sWb")

            mmt = None
            for h in (0, 1):
                ob = h

                def mms(t, h=h, wb=wb, ob=ob):
                    r = None
                    for dc in range(DC):
                        r = t.matmul(pS[:, ob, :], wslab[:, wb, dc, :], xn[:, dc, h * 512:(h + 1) * 512],
                                     start=(dc == 0), stop=(dc == DC - 1))
                    return r
                w = [t_w, evp.get((idx - 1, h), self.tok["pA_free"])]
                if idx == 0:
                    w += [xn_h[h]]
                mmt = self.op("tensor", mms, waits=w)

                t_e = self.op("vector", lambda v, h=h, ob=ob, isv=isv, wb=wb: v.tensor_copy(
                    (env["vstage"] if isv else env["kstage"])[:, h * 512:(h + 1) * 512], pS[:, ob, :]),
                    waits=[mmt, evp.get((idx - 1, h), V0),
                           kv_store.get((idx - 1, h), O0),
                           vtr_done.get(idx - 1, V0),
                           self.tok.get("k2_%d" % (g - 1), V0) if not isv else V0])
                evp[(idx, h)] = t_e
            slab_mm[idx] = mmt

            if not isv:
                t_s = self.op("act", lambda sc, g=g: sc.dma_start(
                    env["kv_cont"][g, :, :], env["kstage"][:, :]),
                    waits=[evp[(idx, 0)], evp[(idx, 1)]], by=16, sem="sOK")
                kv_store[(idx, 0)] = kv_store[(idx, 1)] = t_s
                def flush_kn(g2):
                    def knmm(t):
                        t.matmul(pS[0:1, 2, :], env["ones_bf"][:, :], env["nsq"][:, 0:512], start=True, stop=True)
                        return t.matmul(pS[0:1, 3, :], env["ones_bf"][:, :], env["nsq"][:, 512:1024], start=True, stop=True)
                    t_kn = self.op("tensor", knmm,
                                   waits=[self.tok["k2_%d" % g2],
                                          self.tok.get("knred_%d" % (g2 - 1),
                                                       self.tok["pB_free"] if g2 == 0 else T0)])
                    self.tok["knmm_%d" % g2] = t_kn

                    def knred(v):
                        v.tensor_reduce(out=env["scr2"][0:1, 8:9], in_=pS[0:1, 2, :], axis=AX.X, op=ALU.max)
                        return v.tensor_reduce(out=env["scr2"][0:1, 9:10], in_=pS[0:1, 3, :], axis=AX.X, op=ALU.max)
                    t_kr = self.op("vector", knred, waits=[t_kn, self.tok.get("knred_%d" % (g2 - 1), V0)])
                    t_kr2 = self.op("vector", lambda v: v.tensor_reduce(
                        out=env["scr2"][0:1, 10:11], in_=env["scr2"][0:1, 8:10], axis=AX.X, op=ALU.max),
                        waits=[t_kr])
                    t_kr3 = self.op("vector", lambda v: v.tensor_tensor(
                        out=env["kn2max"][:, :], in0=env["kn2max"][:, :], in1=env["scr2"][0:1, 10:11], op=ALU.max),
                        waits=[t_kr2, self.tok["norminit"]])
                    self.tok["knred_%d" % g2] = t_kr3
                    return t_kr3

                if g > 0:
                    flush_kn(g - 1)
                t_k2 = self.op("vector", lambda v: v.tensor_tensor(
                    out=env["nsq"], in0=env["kstage"][:, :], in1=env["kstage"][:, :], op=ALU.mult),
                    waits=[evp[(idx, 0)], evp[(idx, 1)], self.tok.get("knmm_%d" % (g - 1), T0)])
                self.tok["k2_%d" % g] = t_k2
                if g == NKV - 1:
                    t_kr3 = flush_kn(g)
                    t_z = self.op("vector", lambda v: v.memset(env["vtstage"][:, :, :], 0.0),
                                  waits=[t_kr3, vtst_free])
                    t_knc = self.op("vector", lambda v: v.tensor_copy(
                        env["vtstage"][0:1, 0, 0:1], env["kn2max"][:, :]), waits=[t_z])
                    t_kns = self.op("act", lambda sc: sc.dma_start(
                        env["kv_cont"][16, :, :], env["vtstage"][:, :, :].rearrange("p a b -> p (a b)")),
                        waits=[t_knc, t_s], by=16, sem="sOK")
                    kv_store[(idx, 0)] = kv_store[(idx, 1)] = t_kns
                    self.tok["pB_free"] = t_kr3
                    self.tok["vt_zeroed"] = t_kns
            else:
                prev_tt = {}
                last_ev = None
                for tt in range(NROW):
                    def vtr1(t, tt=tt):
                        return t.transpose(env["pV"][:, tt % 2, 0:128], env["vstage"][:, tt * 128:(tt + 1) * 128],
                                           env["identb_sb"][:, :])
                    w = [evp[(idx, 0)], evp[(idx, 1)], self.tok["consts"]]
                    if tt < 2:
                        w.append(self.tok.get("pV_free", V0))
                    else:
                        w.append(prev_tt[tt - 2])
                    t_tr = self.op("tensor", vtr1, waits=w)
                    t_ev = self.op("vector", lambda v, tt=tt: v.tensor_copy(
                        env["vtstage"][:, tt, :], env["pV"][:, tt % 2, 0:128]),
                        waits=[t_tr, vtst_free, self.tok.get("vt_zeroed", O0)])
                    prev_tt[tt] = t_ev
                    last_ev = t_ev
                vtr_done[idx] = last_ev
                self.tok["pV_free"] = last_ev

                def vst(sc, s, g=g):
                    for tt in range(NROW):
                        sc.dma_start(env["kv_cont"][NKV + tt, :, g * 128:(g + 1) * 128],
                                     env["vtstage"][:, tt, :]).then_inc(s, 16)
                t_s = self.multi("act", vst, waits=[last_ev], n=NROW, sem="sOV")
                vtst_free = t_s
                kv_store[(idx, 0)] = kv_store[(idx, 1)] = t_s
        kv_all_stored = kv_store[(2 * NKV - 1, 0)]
        self.tok["pA_free"] = evp[(2 * NKV - 1, 1)]

        t_cc = self.op("gpsimd", lambda g: g.collective_compute(
            "AllGather", ALU.bypass,
            replica_groups=[[0, 1, 2, 3], [4, 5, 6, 7]],
            ins=[env["kv_cont"][:, :, :]],
            outs=[env["kv_gath"][:, :, :, :]]),
            waits=[kv_all_stored], sem="sC")

        owner = {}
        for rr in range(4):
            for ss, gg in enumerate(rows_for_rank(rr)):
                owner[gg] = (rr, ss)

        def reorder(g, s):
            for j in range(32):
                rr, ss = owner[j]
                g.dma_start(env["k_seq"][:, :, j * 128:(j + 1) * 128],
                            env["kv_gath"][rr, 0:NKV, :, ss * 128:(ss + 1) * 128]).then_inc(s, 16)
                g.dma_start(env["v_seq"][j, :, :],
                            env["kv_gath"][rr, NKV + ss, :, :]).then_inc(s, 16)
        t_reord = self.multi("sync", reorder, waits=[t_cc], n=64, sem="sRE")

        qs_store = {}
        qevd = {}
        q2_done = {}
        qn_mm = {}
        qn_red = {}
        for qh in range(NH):
            idx = 2 * NKV + qh
            wb = idx % 2
            t_w = self.op("gpsimd", lambda gp, qh=qh, wb=wb: gp.dma_start(
                wslab[:, wb, :, :], env["wq_s"][qh, :, :, :]),
                waits=[slab_mm.get(idx - 2, self.tok["wslab_free"])], by=16,
                sem="sWa" if wb == 0 else "sWb")

            mmt = None
            for h in (0, 1):
                ob = h

                def mms(t, h=h, wb=wb, ob=ob):
                    r = None
                    for dc in range(DC):
                        r = t.matmul(pS[:, ob, :], wslab[:, wb, dc, :], xn[:, dc, h * 512:(h + 1) * 512],
                                     start=(dc == 0), stop=(dc == DC - 1))
                    return r
                w = [t_w, qevd.get((qh - 1, h), evp.get((2 * NKV - 1, h), V0))]
                if qh == 0:
                    w.append(self.tok["pA_free"])
                mmt = self.op("tensor", mms, waits=w)
                t_e = self.op("vector", lambda v, h=h, ob=ob, qh=qh: v.tensor_copy(
                    env["qstage"][:, qh % 2, h * 512:(h + 1) * 512], pS[:, ob, :]),
                    waits=[mmt, qs_store.get(qh - 2, O0), q2_done.get(qh - 2, V0)])
                qevd[(qh, h)] = t_e
            slab_mm[idx] = mmt

            t_s = self.op("act", lambda sc, qh=qh: sc.dma_start(
                env["q_dram"][qh, :, :], env["qstage"][:, qh % 2, :]),
                waits=[qevd[(qh, 0)], qevd[(qh, 1)]], by=16,
                sem="sQa" if qh % 2 == 0 else "sQb")
            qs_store[qh] = t_s

            def flush_qn(qh2):
                def qnmm(t):
                    t.matmul(pS[0:1, 2, :], env["ones_bf"][:, :], env["hw_sb"][:, 0:512], start=True, stop=True)
                    return t.matmul(pS[0:1, 3, :], env["ones_bf"][:, :], env["hw_sb"][:, 512:1024], start=True, stop=True)
                t_qn = self.op("tensor", qnmm, waits=[q2_done[qh2], qn_red.get(qh2 - 1, self.tok["pB_free"])])
                qn_mm[qh2] = t_qn

                def qred(v):
                    v.tensor_reduce(out=env["scr2"][0:1, 0:1], in_=pS[0:1, 2, :], axis=AX.X, op=ALU.max)
                    return v.tensor_reduce(out=env["scr2"][0:1, 1:2], in_=pS[0:1, 3, :], axis=AX.X, op=ALU.max)
                t_qr = self.op("vector", qred, waits=[t_qn, self.tok["norminit"], qn_red.get(qh2 - 1, V0)])
                t_qr2 = self.op("vector", lambda v: v.tensor_reduce(
                    out=env["scr2"][0:1, 2:3], in_=env["scr2"][0:1, 0:2], axis=AX.X, op=ALU.max), waits=[t_qr])
                t_qr3 = self.op("vector", lambda v: v.tensor_tensor(
                    out=env["qn2max"][:, :], in0=env["qn2max"][:, :], in1=env["scr2"][0:1, 2:3], op=ALU.max), waits=[t_qr2])
                qn_red[qh2] = t_qr3

            if qh > 0:
                flush_qn(qh - 1)
            t_q2 = self.op("vector", lambda v, qh=qh: v.tensor_tensor(
                out=env["nsq"], in0=env["qstage"][:, qh % 2, :], in1=env["qstage"][:, qh % 2, :], op=ALU.mult),
                waits=[qevd[(qh, 0)], qevd[(qh, 1)], qn_mm.get(qh - 1, self.tok["xn_free"])])
            q2_done[qh] = t_q2
            if qh == NH - 1:
                flush_qn(qh)
        self.tok["pB_free"] = qn_red[NH - 1]
        self.tok["pA_free"] = qevd[(NH - 1, 1)]
        self.tok["xn_free"] = slab_mm[2 * NKV + NH - 1]
        self.tok["wslab_free"] = slab_mm[2 * NKV + NH - 1]
        qn_done = qn_red[NH - 1]
        qproj_mm_last = slab_mm[2 * NKV + NH - 1]

        t_knl = self.op("sync", lambda sy: sy.dma_start(
            env["knstage"][0:1, 0:4],
            env["kv_gath"][0:4, 16:17, 0:1, 0:1].rearrange("r a p t -> (a p) (r t)"),
            allow_slow_non_contiguous=True),
            waits=[t_cc, ("sOK", self.cnt.get("sOK", 0))], by=16, sem="sKN")
        t_kn1 = self.op("vector", lambda v: v.tensor_copy(env["scr2"][0:1, 11:15], env["knstage"][0:1, 0:4]),
                        waits=[t_knl])
        kn_done = self.op("vector", lambda v: v.tensor_reduce(
            out=env["kn2max"][:, :], in_=env["scr2"][0:1, 11:15], axis=AX.X, op=ALU.max),
            waits=[t_kn1])

        t = self.op("vector", lambda v: v.tensor_tensor(
            out=env["scr2"][0:1, 4:5], in0=env["qn2max"][:, :], in1=env["kn2max"][:, :], op=ALU.mult),
            waits=[qn_done, kn_done])
        t = self.op("act", lambda a: a.activation(out=env["scr2"][0:1, 5:6], in_=env["scr2"][0:1, 4:5], func=AF.Sqrt), waits=[t])
        t = self.op("vector", lambda v: v.tensor_scalar(
            out=env["scr2"][0:1, 6:7], in0=env["scr2"][0:1, 5:6], scalar1=-SCALE, scalar2=0.0, op0=ALU.mult, op1=ALU.add), waits=[t])
        t_nm = self.op("tensor", lambda te: te.matmul(
            pC[:, 0, 0:1], env["ones_row"][:, :], env["scr2"][0:1, 6:7], start=True, stop=True),
            waits=[t, self.tok["pC_free"]])
        t_nmc = self.op("vector", lambda v: v.tensor_copy(env["negm_sb"][:, :], pC[:, 0, 0:1]), waits=[t_nm])
        self.tok["pC_free"] = t_nmc

        wslab_h = env["wslab"]
        pt2_h = env["pt2"]

        def ptv(kb):
            if kb < 16:
                return bass.AP(wslab_h, kb * 512, [[8192, 128], [1, 512]])
            return pt2_h[:, kb - 16, :]

        def ptv4(j4):
            if 4 * j4 < 16:
                return bass.AP(wslab_h, 4 * j4 * 512, [[8192, 128], [1, 2048]])
            return bass.AP(pt2_h, (4 * j4 - 16) * 512, [[8192, 128], [1, 2048]])

        def stb(kb):
            return pS[:, kb % 4, :]

        mask_used = V0
        sl_free = {("k", 0): T0, ("k", 1): T0, ("v", 0): T0, ("v", 1): T0,
                   ("q", 0): T0, ("q", 1): T0}
        gidx = 0
        bank_free = {0: self.tok["pA_free"], 1: self.tok["pA_free"],
                     2: kn_done, 3: kn_done}
        pc_reader = {0: t_nmc, 1: self.tok["pC_free"]}
        last_exp = A0
        attn_done = None
        pend_lav = []
        pend_fin = None
        pop_last = T0

        def flush_fin():
            nonlocal pend_fin, attn_done
            if pend_fin is None:
                return
            ctx = pend_fin
            pend_fin = None
            cur = ctx["cur"]
            def _recip(v, pl=ctx["pl"]):
                with self.nc.allow_low_precision(reason="1/l in bf16: 0.4% rel on the softmax scale, within tolerance"):
                    return v.reciprocal(out=env["linv_bf"], in_=pC[0:1, pl, :])
            t_linv = self.op("vector", _recip,
                             waits=[cur["t_l"], self.tok.get("lb_last", T0)])
            t_lb = self.op("tensor", lambda te, pl=ctx["pl"]: te.matmul(
                pC[:, pl, :], env["ones_rbf"][:, :], env["linv_bf"], start=True, stop=True),
                waits=[t_linv])
            self.tok["lb_last"] = t_lb
            t_lbc = self.op("vector", lambda v, pl=ctx["pl"]: v.tensor_copy(env["linvb_sb"], pC[:, pl, :]),
                            waits=[t_lb])
            pc_reader[ctx["pl"]] = t_lbc

            def evac(v, g=ctx["g"], i=ctx["i"], pav=ctx["pav"]):
                return v.tensor_tensor(out=xn[:, 4 * g:4 * g + 4, i * 128:(i + 1) * 128],
                                       in0=pC[:, pav, :], in1=env["linvb_sb"], op=ALU.mult)
            t_ev = self.op("vector", evac, waits=[cur["t_av"], t_lbc, qproj_mm_last])
            pc_reader[ctx["pav"]] = t_ev
            sl_free[("v", 0)] = cur["t_av"]
            attn_done = t_ev

        for i in range(NROW):
            NB = 4 * (i + 1)
            L = 512 * (i + 1)
            mw = [mask_used]
            if i == 0:
                mw += [q2_done[NH - 1], qs_store[NH - 1], qs_store[NH - 2]]
            t_mask = self.op("sync", lambda sy, i=i: sy.dma_start(
                env["masks_sb"], env["masks"][i, :, :, :]),
                waits=mw, by=16, sem="sMK")
            mask_last = None

            for g in range(NKV):
                pav = g % 2
                pl = 1 - pav
                par = gidx % 2
                gidx += 1
                t_kld = self.op("sync", lambda sy, g=g, L=L, par=par: sy.dma_start(
                    env["ksl"][:, par, 0:L], env["k_seq"][g, :, 0:L]),
                    waits=[sl_free[("k", par)], t_reord], by=16,
                    sem="sKL" if par == 0 else "sKLb")
                t_qld = self.op("sync", lambda sy, g=g, i=i, par=par: sy.dma_start(
                    bass.AP(env["qsl"], par * 512, [[1024, 128], [128, 4], [1, 128]]),
                    env["q_dram"][4 * g:4 * g + 4, :, i * 128:(i + 1) * 128].rearrange("h p t -> p h t")),
                    waits=[qs_store[4 * g + 3], qs_store[4 * g + 2], sl_free[("q", par)]],
                    by=16, sem="sQL" if par == 0 else "sQLb")

                pt_ready = {}
                t_st = None
                cur = {"t_l": None, "t_av": None, "t_vld": None}

                def emit_vld(g=g, NB=NB, cur=cur):
                    cur["t_vld"] = self.op("sync", lambda sy, g=g, NB=NB: sy.dma_start(
                        env["vsl"][:, 0:NB, :],
                        env["v_seq"][0:NB, :, g * 128:(g + 1) * 128].rearrange("b p v -> p b v")),
                        waits=[sl_free[("v", 0)], t_reord], by=16, sem="sVL")

                def emit_lav(kb, NB=NB, pav=pav, pl=pl, pt_ready=pt_ready, cur=cur):
                    lw = [pt_ready[kb]]
                    if kb == 0:
                        lw.append(pc_reader[pl])
                    cur["t_l"] = self.op("tensor", lambda t, kb=kb, NB=NB, pl=pl: t.matmul(
                        pC[0:1, pl, :], env["ones_bf"][:, :], ptv(kb),
                        start=(kb == 0), stop=(kb == NB - 1), skip_group_check=True), waits=lw)
                    aw = [pt_ready[kb]]
                    if kb == 0:
                        aw += [cur["t_vld"], pc_reader[pav]]
                    cur["t_av"] = self.op("tensor", lambda t, kb=kb, NB=NB, pav=pav: t.matmul(
                        pC[:, pav, :], env["vsl"][:, kb, :], ptv(kb),
                        start=(kb == 0), stop=(kb == NB - 1), skip_group_check=True), waits=aw)

                if not pend_lav:
                    flush_fin()
                    emit_vld()
                for kb in range(NB):
                    def stmm(t, kb=kb, par=par):
                        return t.matmul(stb(kb), env["ksl"][:, par, kb * 128:(kb + 1) * 128],
                                        bass.AP(env["qsl"], par * 512, [[1024, 128], [128, 4], [1, 128]]), start=True, stop=True,
                                        skip_group_check=True)
                    t_st = self.op("tensor", stmm, waits=[t_kld, t_qld, bank_free[kb % 4]])
                    if pend_lav:
                        pend_lav.pop(0)()
                        pop_last = pend_fin["cur"]["t_av"]
                        if not pend_lav:
                            flush_fin()
                            emit_vld()

                    if kb % 4 == 3:
                        j4 = kb // 4

                        def expf(a, j4=j4):
                            return a.activation(out=ptv4(j4), in_=pS[:, :, :],
                                                func=AF.Exp, bias=env["negm_sb"][:, :], scale=SCALE)
                        t_ex = self.op("act", expf, waits=[t_st, t_nmc, pop_last])
                        last_exp = t_ex
                        for b in range(4):
                            bank_free[b] = t_ex
                        for kk in range(4 * j4, 4 * j4 + 4):
                            pt_ready[kk] = t_ex
                            if kk >= NB - 4:
                                jj = kk - (NB - 4)
                                t_mk = self.op("vector", lambda v, kk=kk, jj=jj: v.tensor_tensor(
                                    out=ptv(kk), in0=ptv(kk),
                                    in1=bass.AP(env["qstage"], jj * 512, [[2048, 128], [1, 512]]),
                                    op=ALU.mult),
                                    waits=[t_ex, t_mask])
                                mask_last = t_mk
                                pt_ready[kk] = t_mk
                        if j4 >= 2:
                            for kk in range(4 * (j4 - 2), 4 * (j4 - 2) + 4):
                                emit_lav(kk)
                for kbx in range(max(0, NB - 8), NB):
                    pend_lav.append(lambda kb=kbx, f=emit_lav: f(kb))
                pend_fin = {"g": g, "i": i, "pav": pav, "pl": pl, "cur": cur}

                sl_free[("k", par)] = t_st
                sl_free[("q", par)] = t_st
            mask_used = mask_last
        while pend_lav:
            pend_lav.pop(0)()
        flush_fin()

        self.tok["pB_free"] = last_exp
        self.tok["pC_free"] = attn_done
        self.tok["wslab_free"] = attn_done

        evd = {}
        std = {}
        slab_mm2 = {}
        for oc in range(DC):
            wb = oc % 2
            t_w = self.op("gpsimd", lambda g, oc=oc, wb=wb: g.dma_start(
                wslab[:, wb, :, :], env["wo_s"][oc, :, :, :]),
                waits=[slab_mm2.get(oc - 2, self.tok["wslab_free"])], by=16,
                sem="sWa" if wb == 0 else "sWb")
            for h in (0, 1):
                ob = h

                def mms(t, oc=oc, h=h, wb=wb, ob=ob):
                    r = None
                    for dc in range(DC):
                        r = t.matmul(pS[:, ob, :], wslab[:, wb, dc, :], xn[:, dc, h * 512:(h + 1) * 512],
                                     start=(dc == 0), stop=(dc == DC - 1))
                    return r
                w = [t_w, evd.get((oc - 1, h), V0)]
                if oc == 0:
                    w.append(attn_done)
                t_mm = self.op("tensor", mms, waits=w)
                slab_mm2[oc] = t_mm

                t_r = self.op("sync", lambda sy, oc=oc, h=h, ob=ob: sy.dma_start(
                    env["resch"][:, ob, :], x_res[oc, :, h * 512:(h + 1) * 512]),
                    waits=[evd.get((oc - 1, h), self.tok["resch_free"]),
                           self.tok["x_src_chunks"].get((oc, h), V0)],
                    by=16, sem="sRa" if ob == 0 else "sRb")

                t_e2 = self.op("vector", lambda v, ob=ob: v.tensor_tensor(
                    out=env["x1buf"][:, ob, :], in0=pS[:, ob, :], in1=env["resch"][:, ob, :], op=ALU.add),
                    waits=[t_mm, t_r, std.get((oc - 1, h), self.tok["x1b_free"][h]),
                           evd.get((oc - 1, h), V0)])
                evd[(oc, h)] = t_e2

                t_s = self.op("act", lambda sc, oc=oc, h=h, ob=ob: sc.dma_start(
                    dst[oc, :, h * 512:(h + 1) * 512], env["x1buf"][:, ob, :]),
                    waits=[t_e2], by=16, sem="sOa" if ob == 0 else "sOb")
                std[(oc, h)] = t_s
            if next_src is not None:
                if oc >= 4:
                    self.emit_stats_mm(oc - 4)
                if oc >= 2:
                    self.emit_stats_load(next_src, oc - 2, std)
        if next_src is not None:
            for dcx in (DC - 2, DC - 1):
                self.emit_stats_mm(dcx - 2)
                self.emit_stats_load(next_src, dcx, std)
            for dcx in (DC - 2, DC - 1):
                self.emit_stats_mm(dcx)

        self.tok["pA_free"] = evd[(DC - 1, 1)]
        self.tok["xn_free"] = slab_mm2[DC - 1]
        self.tok["wslab_free"] = slab_mm2[DC - 1]
        self.tok["resch_free"] = evd[(DC - 1, 1)]
        self.tok["x1b_free"] = {0: std[(DC - 1, 0)], 1: std[(DC - 1, 1)]}
        self.tok["last_store"] = [std[(DC - 1, 0)], std[(DC - 1, 1)]]
        self.tok["x_src"] = [std[(DC - 1, 0)], std[(DC - 1, 1)]]
        self.tok["x_src_chunks"] = dict(std)

    def emit(self):
        q = self.q
        blk = self.env["blk"]

        @blk.sync
        def _(eng):
            for f in q["sync"]:
                f(eng)

        @blk.scalar
        def _(eng):
            for f in q["act"]:
                f(eng)

        @blk.gpsimd
        def _(eng):
            for f in q["gpsimd"]:
                f(eng)

        @blk.tensor
        def _(eng):
            for f in q["tensor"]:
                f(eng)

        @blk.vector
        def _(eng):
            for f in q["vector"]:
                f(eng)


def build_program():
    nc = bass.Bass("TRN2", target_bir_lowering=False)
    env = {}

    def din(name, shape, dt=F32):
        env[name] = nc.dram_tensor(name, shape, dt, kind="ExternalInput").ap()

    din("x_in", [DC, 128, TOK])
    din("wq_s", [NH, 128, DC, 128])
    din("wk_s", [NKV, 128, DC, 128])
    din("wv_s", [NKV, 128, DC, 128])
    din("wo_s", [DC, 128, DC, 128])
    din("bw_s", [DC, 128, DC, 128])
    din("bb_s", [128, DC])
    din("rw_s", [128, DC, NEXP])
    din("rb_s", [NEXP, 1])
    din("a_s", [128, DC, 128])
    din("b2_s", [128, DC, 128])
    din("emap", [NEXP, 1], I32)
    din("iota8", [NEXP, E])
    din("e16", [E, 128])
    din("ident32", [128, 128])
    din("identb", [128, 128], BF16)
    din("masks", [NROW, 128, 4, 512], BF16)

    env["out_x"] = nc.dram_tensor("out_x", [DC, 128, TOK], F32, kind="ExternalOutput").ap()
    env["x_res"] = nc.dram_tensor("x_res", [DC, 128, TOK], F32).ap()
    env["q_dram"] = nc.dram_tensor("q_dram", [NH, 128, TOK], BF16).ap()
    env["kv_cont"] = nc.dram_tensor("kv_cont", [17, 128, TOK], BF16).ap()
    env["kv_gath"] = nc.dram_tensor("kv_gath", [4, 17, 128, TOK], BF16).ap()
    env["k_seq"] = nc.dram_tensor("k_seq", [NKV, 128, S], BF16).ap()
    env["v_seq"] = nc.dram_tensor("v_seq", [32, 128, NKV * 128], BF16).ap()

    sb = [
        ("xn", [128, DC, TOK], BF16),
        ("wslab", [128, 2, DC, 128], BF16),
        ("xchunk", [128, 2, TOK], F32),
        ("xbf", [128, 2, TOK], BF16),
        ("x2bf", [128, 2, TOK], BF16),
        ("tmpf", [128, 2, 512], F32),
        ("resch", [128, 2, 512], F32),
        ("x1buf", [128, 2, 512], F32),
        ("ones_bf", [128, 1], BF16),
        ("ones_row", [1, 128], F32),
        ("ones_rbf", [1, 128], BF16),
        ("ident_sb", [128, 128], F32),
        ("identb_sb", [128, 128], BF16),
        ("rows", [1, 1024], F32),
        ("rows2", [1, 1024], F32),
        ("rows3", [1, 1024], F32),
        ("halfb", [128, DC], F32),
        ("rbias_sb", [NEXP, 1], F32),
        ("emap_i", [NEXP, 1], I32),
        ("emap_f", [NEXP, 1], F32),
        ("iota_sb", [NEXP, E], F32),
        ("m_onehot", [NEXP, E], F32),
        ("e16_sb", [E, 128], F32),
        ("rw_sb", [128, DC, NEXP], BF16),
        ("a_sb", [128, DC, 128], BF16),
        ("b_sb", [128, DC, 128], BF16),
        ("probs", [128, NROW, NEXP], F32),
        ("racc", [128, NROW, NEXP], F32),
        ("racc2", [128, NROW, NEXP], F32),
        ("spw", [128, NROW, NEXP], F32),
        ("rmax", [128, NROW], F32),
        ("rsum", [128, NROW], F32),
        ("logit_sb", [NEXP, TOK], F32),
        ("spw_t", [NEXP, TOK], F32),
        ("wrep_sb", [128, TOK], F32),
        ("hw_sb", [128, TOK], BF16),
        ("kstage", [128, TOK], BF16),
        ("vstage", [128, TOK], BF16),
        ("vtstage", [128, NROW, 128], BF16),
        ("qstage", [128, 2, TOK], BF16),
        ("qn2max", [1, 1], F32),
        ("kn2max", [1, 1], F32),
        ("scr2", [1, 16], F32),
        ("negm_sb", [128, 1], F32),
        ("knstage", [1, 4], BF16),
        ("ksl", [128, 2, S], BF16),
        ("vsl", [128, 32, 128], BF16),
        ("qsl", [128, 2, 512], BF16),
        ("pt2", [128, 16, 512], BF16),
    ]
    ps = [("pS", [128, 4, 512], F32), ("pC", [128, 2, 512], F32), ("pV", [128, 2, 1024], BF16)]
    sems = ["sT", "sV", "sA", "sC", "sXa", "sXb", "sRa", "sRb", "sMK", "sKL", "sKLb", "sVL", "sQL", "sQLb", "sKN", "sCN", "sOa", "sOb", "sOK", "sOV", "sQa", "sQb", "sWa", "sWb", "sRE", "sSW"]

    import contextlib
    with contextlib.ExitStack() as ctx:
        for name, shape, dt in sb:
            env[name] = ctx.enter_context(nc.sbuf_tensor(name, shape, dt))
        env["rows4"] = env["logit_sb"][0:1, :]
        env["linv_bf"] = bass.AP(env["vtstage"], 0, [[1024, 1], [1, 512]])
        env["wt_sb"] = env["spw_t"][0:E, :]
        env["masks_sb"] = bass.AP(env["qstage"], 0, [[2048, 128], [512, 4], [1, 512]])
        env["nsq"] = env["hw_sb"][:, :]
        env["linv_row"] = env["spw_t"][0:1, 0:512]
        env["linvb_sb"] = env["wrep_sb"][:, 0:512]
        for name, shape, pdt in ps:
            env[name] = ctx.enter_context(nc.psum_tensor(name, shape, pdt))
        for name in sems:
            env[name] = ctx.enter_context(nc.semaphore(name))
        env["blk"] = ctx.enter_context(nc.Block())
        p = P(nc, env)
        p.plan()
        p.emit()
    return nc



def _slab(w):
    din, dout = w.shape
    return np.ascontiguousarray(
        np.asarray(w, dtype=np.float32).reshape(din // 128, 128, dout // 128, 128).transpose(2, 1, 0, 3))


def _host_shared(wq, wk, wv, wo, base_W, base_b, router_W, router_b, A, Bm, emap):
    import ml_dtypes
    d = {}
    wq = np.asarray(wq, dtype=np.float32).reshape(D, NH, HD)
    d["wq_s"] = np.ascontiguousarray(
        wq.reshape(DC, 128, NH, HD).transpose(2, 1, 0, 3))
    wk = np.asarray(wk, dtype=np.float32).reshape(D, NKV, HD)
    d["wk_s"] = np.ascontiguousarray(wk.reshape(DC, 128, NKV, HD).transpose(2, 1, 0, 3))
    wv = np.asarray(wv, dtype=np.float32).reshape(D, NKV, HD)
    d["wv_s"] = np.ascontiguousarray(wv.reshape(DC, 128, NKV, HD).transpose(2, 1, 0, 3))
    d["wo_s"] = _slab(wo)
    bw = np.asarray(base_W, dtype=np.float32)
    d["bw_s"] = np.ascontiguousarray(bw.reshape(DC, 128, DC, 128).transpose(0, 3, 2, 1))
    d["bb_s"] = np.ascontiguousarray(np.asarray(base_b, dtype=np.float32).reshape(DC, 128).T)
    rw = np.asarray(router_W, dtype=np.float32)
    d["rw_s"] = np.ascontiguousarray(rw.T.reshape(DC, 128, NEXP).transpose(1, 0, 2))
    d["rb_s"] = np.ascontiguousarray(np.asarray(router_b, dtype=np.float32).reshape(NEXP, 1))
    Aa = np.asarray(A, dtype=np.float32)
    d["a_s"] = np.ascontiguousarray(Aa.reshape(E * R, D).T.reshape(DC, 128, E * R).transpose(1, 0, 2))
    Bb = np.asarray(Bm, dtype=np.float32)
    d["b2_s"] = np.ascontiguousarray(
        Bb.transpose(0, 2, 1).reshape(E * R, DC, 128))
    d["emap"] = np.ascontiguousarray(np.asarray(emap, dtype=np.int32).reshape(NEXP, 1))
    d["iota8"] = np.ascontiguousarray(np.tile(np.arange(E, dtype=np.float32), (NEXP, 1)))
    e16 = np.zeros((E, 128), dtype=np.float32)
    for e in range(E):
        e16[e, e * R:(e + 1) * R] = 2.0
    d["e16"] = e16
    d["ident32"] = np.eye(128, dtype=np.float32)
    d["identb"] = np.eye(128, dtype=ml_dtypes.bfloat16)
    return d


def _mask_for_rank(r):
    import ml_dtypes
    rows = rows_for_rank(r)
    m = np.zeros((NROW, 4, 128, 512), dtype=np.float32)
    for i, g in enumerate(rows):
        qpos = g * 128 + np.arange(128)
        for j in range(4):
            kpos = (4 * i + j) * 128 + np.arange(128)
            vis = (kpos[:, None] <= qpos[None, :]).astype(np.float32)
            m[i, j] = np.tile(vis, (1, 4))
    m = m.transpose(0, 2, 1, 3)
    return np.ascontiguousarray(m.astype(ml_dtypes.bfloat16))


_NC_CACHE = {}


def kernel(L0_attn_x, L0_attn_wq, L0_attn_wk, L0_attn_wv, L0_attn_wo,
           L0_ffn_base_W, L0_ffn_base_b, L0_ffn_router_W, L0_ffn_router_b,
           L0_ffn_A, L0_ffn_B, L0_ffn_expert_map):
    x = np.asarray(L0_attn_x, dtype=np.float32)
    shared = _host_shared(L0_attn_wq, L0_attn_wk, L0_attn_wv, L0_attn_wo,
                          L0_ffn_base_W, L0_ffn_base_b, L0_ffn_router_W,
                          L0_ffn_router_b, L0_ffn_A, L0_ffn_B, L0_ffn_expert_map)
    if "nc" not in _NC_CACHE:
        _NC_CACHE["nc"] = build_program()
    nc = _NC_CACHE["nc"]

    in_maps = []
    meta = []
    for c in range(8):
        b, r = c // 4, c % 4
        rows = rows_for_rank(r)
        meta.append((b, rows))
        xi = np.concatenate([x[b, g * 128:(g + 1) * 128, :] for g in rows], axis=0)
        xi = np.ascontiguousarray(xi.T.reshape(DC, 128, TOK))
        im = dict(shared)
        im["x_in"] = xi
        im["masks"] = _mask_for_rank(r)
        in_maps.append(im)

    res = run_bass_kernel_spmd(nc, in_maps, core_ids=list(range(8)))

    out = np.empty((B, S, D), dtype=np.float32)
    for c in range(8):
        b, rows = meta[c]
        oc = np.asarray(res.results[c]["out_x"])
        ocv = np.ascontiguousarray(oc.reshape(D, TOK).T)
        for i, g in enumerate(rows):
            out[b, g * 128:(g + 1) * 128, :] = ocv[i * 128:(i + 1) * 128, :]
    return out


# version 17
# speedup vs baseline: 1.0497x; 1.0497x over previous
import numpy as np
import concourse.bass as bass
import concourse.mybir as mybir
from concourse.bass_utils import run_bass_kernel_spmd

F32 = mybir.dt.float32
BF16 = mybir.dt.bfloat16
I32 = mybir.dt.int32
AF = mybir.ActivationFunctionType
ALU = mybir.AluOpType
AX = mybir.AxisListType

B, S, D = 2, 4096, 4096
NH, NKV, HD = 32, 8, 128
SCALE = HD ** -0.5
TOK = 1024
NROW = 8
DC = D // 128
E, R = 8, 16
NEXP = 15
EPS = 1e-5

N_SUB = 3

T0 = ("sT", 0)
V0 = ("sV", 0)
A0 = ("sA", 0)
O0 = ("sO", 0)


def rows_for_rank(r):
    return sorted({r, 7 - r, 8 + r, 15 - r, 16 + r, 23 - r, 24 + r, 31 - r})


class P:

    def __init__(self, nc, env):
        self.nc = nc
        self.env = env
        self.q = {k: [] for k in ("sync", "act", "gpsimd", "tensor", "vector")}
        self.cnt = {}
        self.tok = {}

    def op(self, eng, emit, waits=(), sem=None, by=1):
        if sem is None:
            sem = {"act": "sA", "tensor": "sT", "vector": "sV"}[eng]
        s = self.env[sem]
        self.cnt[sem] = self.cnt.get(sem, 0) + by
        tgt = self.cnt[sem]
        ws = tuple((self.env[ks], v) for (ks, v) in waits if v > 0)

        def f(e, emit=emit, ws=ws, s=s, by=by):
            for (sm, v) in ws:
                e.wait_ge(sm, v)
            r = emit(e)
            if r is not None and by > 0:
                r.then_inc(s, by)
        self.q[eng].append(f)
        return (sem, tgt)

    def multi(self, eng, emit, waits=(), sem=None, n=1, by=16):
        s = self.env[sem]
        ws = tuple((self.env[ks], v) for (ks, v) in waits if v > 0)

        def f(e, emit=emit, ws=ws, s=s):
            for (sm, v) in ws:
                e.wait_ge(sm, v)
            emit(e, s)
        self.q[eng].append(f)
        self.cnt[sem] = self.cnt.get(sem, 0) + n * by
        return (sem, self.cnt[sem])

    def plan(self):
        env = self.env
        self.prologue()
        self.tok["x_src"] = []
        self.tok["x_src_chunks"] = {}
        self.ffn_sublayer(src=env["x_in"], dst=env["x_res"] if N_SUB > 1 else env["out_x"],
                          next_src=env["x_res"] if N_SUB > 1 else None)
        if N_SUB >= 2:
            self.attn_sublayer(dst=env["x_res"] if N_SUB > 2 else env["out_x"],
                               next_src=env["x_res"] if N_SUB > 2 else None)
        if N_SUB >= 3:
            self.ffn_sublayer(src=env["x_res"], dst=env["out_x"])
        self.op("act", lambda a: a.nop(), waits=list(self.tok["last_store"]), by=0)

    def prologue(self):
        env = self.env
        t_ones = self.op("vector", lambda v: v.memset(env["ones_bf"][:, :], 1.0))
        t_ones = self.op("vector", lambda v: v.memset(env["ones_row"][:, :], 1.0))
        t_ones = self.op("vector", lambda v: v.memset(env["ones_rbf"][:, :], 1.0))
        t_ones = self.op("vector", lambda v: v.memset(env["qn2max"][:, :], 0.0))
        t_ones = self.op("vector", lambda v: v.memset(env["kn2max"][:, :], 0.0))
        self.tok["ones"] = t_ones
        self.tok["norminit"] = t_ones

        def consts(sy, s):
            sy.dma_start(env["ident_sb"][:, :], env["ident32"][:, :]).then_inc(s, 16)
            sy.dma_start(env["identb_sb"][:, :], env["identb"][:, :]).then_inc(s, 16)
            sy.dma_start(env["rbias_sb"][:, :], env["rb_s"][:, :]).then_inc(s, 16)
            sy.dma_start(env["iota_sb"][:, :], env["iota8"][:, :]).then_inc(s, 16)
            sy.dma_start(env["halfb"][:, :], env["bb_s"][:, :]).then_inc(s, 16)
            sy.dma_start(env["e16_sb"][:, :], env["e16"][:, :]).then_inc(s, 16)
            sy.dma_start(env["emap_i"][:, :], env["emap"][:, :]).then_inc(s, 16)
        t_consts = self.multi("sync", consts, n=7, sem="sCN")
        self.tok["consts"] = t_consts

        def gp(g, s):
            g.dma_start(env["rw_sb"][:, :, :], env["rw_s"][:, :, :]).then_inc(s, 16)
            g.dma_start(env["a_sb"][:, :, :], env["a_s"][:, :, :]).then_inc(s, 16)
            g.dma_start(env["b_sb"][:, :, :], env["b2_s"][:, :, :]).then_inc(s, 16)
        t_gp = self.multi("gpsimd", gp, n=3, sem="sSW")
        self.tok["small_w"] = t_gp

        t = self.op("vector", lambda v: v.tensor_copy(env["emap_f"][:, :], env["emap_i"][:, :]),
                    waits=[t_consts])
        t_oneh = self.op("vector", lambda v: v.tensor_scalar(
            out=env["m_onehot"][:, :], in0=env["iota_sb"][:, :],
            scalar1=env["emap_f"][:, :], scalar2=0.0, op0=ALU.is_equal, op1=ALU.add),
            waits=[t])
        t_hb = self.op("vector", lambda v: v.tensor_scalar(
            out=env["halfb"][:, :], in0=env["halfb"][:, :],
            scalar1=0.5, scalar2=0.0, op0=ALU.mult, op1=ALU.add),
            waits=[t_consts])
        self.tok["pro_v"] = t_hb
        self.tok["oneh"] = t_oneh
        for k in ("xn_free", "pA_free_T", "wslab_free"):
            self.tok[k] = T0
        for k in ("pA_free", "pB_free", "pC_free", "pD_free",
                  "resch_free", "tmpf_free"):
            self.tok[k] = V0
        self.tok["x1b_free"] = {0: V0, 1: V0}

    def emit_stats_load(self, src, dc, store_toks):
        env = self.env
        xch, xbf, x2bf = env["xchunk"], env["xbf"], env["x2bf"]
        bf = dc % 2
        xch_user = self.tok["xch_user"]
        xbf_user = self.tok["xbf_user"]
        pre = self.tok.setdefault("pre_stats", {})
        sq = self.tok.setdefault("pre_sq", {})
        t_ld = self.op("sync", lambda sy, dc=dc, bf=bf: sy.dma_start(
            xch[:, bf, :], src[dc, :, :]),
            waits=[xch_user[bf], store_toks.get((dc, 0), V0), store_toks.get((dc, 1), V0)],
            by=16, sem="sXa" if bf == 0 else "sXb")
        t_c1 = self.op("vector", lambda v, bf=bf: v.tensor_copy(xbf[:, bf, :], xch[:, bf, :]),
                       waits=[t_ld, self.tok["ones"], xbf_user[bf]])
        t_c2 = self.op("act", lambda a, bf=bf: a.activation(
            out=x2bf[:, bf, :], in_=xbf[:, bf, :], func=AF.Square),
            waits=[t_c1, pre.get(dc - 2, self.tok.get("x2bf_free", T0))])
        sq[dc] = t_c2
        xch_user[bf] = t_c1

    def emit_stats_mm(self, dc):
        env = self.env
        xbf, x2bf = env["xbf"], env["x2bf"]
        pB = env["pB"]
        bf = dc % 2
        pre = self.tok["pre_stats"]

        def mm(t, dc=dc, bf=bf):
            st, sp = dc == 0, dc == DC - 1
            t.matmul(pB[0:1, 0, :], env["ones_bf"][:, :], xbf[:, bf, 0:512], start=st, stop=sp, skip_group_check=True)
            t.matmul(pB[0:1, 1, :], env["ones_bf"][:, :], xbf[:, bf, 512:1024], start=st, stop=sp, skip_group_check=True)
            t.matmul(pB[32:33, 0, :], env["ones_bf"][:, :], x2bf[:, bf, 0:512], start=st, stop=sp, skip_group_check=True)
            return t.matmul(pB[32:33, 1, :], env["ones_bf"][:, :], x2bf[:, bf, 512:1024], start=st, stop=sp, skip_group_check=True)
        w = [self.tok["pre_sq"][dc]]
        if dc == 0:
            w.append(self.tok["pB_free"])
        pre[dc] = self.op("tensor", mm, waits=w)
        self.tok["xbf_user"][bf] = pre[dc]

    def ln_pass(self, src):
        env = self.env
        xch, xbf, x2bf = env["xchunk"], env["xbf"], env["x2bf"]
        pB, pC = env["pB"], env["pC"]
        rows, rows2, rows3, rows4 = env["rows"], env["rows2"], env["rows3"], env["rows4"]
        xch_user = self.tok.setdefault("xch_user", {0: V0, 1: V0})
        xbf_user = self.tok.setdefault("xbf_user", {0: T0, 1: T0})

        mm_done = self.tok.pop("pre_stats", None)
        self.tok.pop("pre_sq", None)
        for dc in (() if mm_done is not None else range(DC)):
            bf = dc % 2
            ck = self.tok["x_src_chunks"]
            t_ld = self.op("sync", lambda sy, dc=dc, bf=bf: sy.dma_start(
                xch[:, bf, :], src[dc, :, :]),
                waits=[xch_user[bf], ck.get((dc, 0), V0), ck.get((dc, 1), V0)], by=16,
                sem="sXa" if bf == 0 else "sXb")
            t_c1 = self.op("vector", lambda v, bf=bf: v.tensor_copy(xbf[:, bf, :], xch[:, bf, :]),
                           waits=[t_ld, self.tok["ones"], xbf_user[bf]])
            t_c2 = self.op("act", lambda a, bf=bf: a.activation(
                out=x2bf[:, bf, :], in_=xbf[:, bf, :], func=AF.Square),
                waits=[t_c1, (mm_done or {}).get(dc - 2, self.tok.get("x2bf_free", T0))])
            cast_done = t_c2
            xch_user[bf] = t_c1

            def mm(t, dc=dc, bf=bf):
                st, sp = dc == 0, dc == DC - 1
                t.matmul(pB[0:1, 0, :], env["ones_bf"][:, :], xbf[:, bf, 0:512], start=st, stop=sp)
                t.matmul(pB[0:1, 1, :], env["ones_bf"][:, :], xbf[:, bf, 512:1024], start=st, stop=sp)
                t.matmul(pB[32:33, 0, :], env["ones_bf"][:, :], x2bf[:, bf, 0:512], start=st, stop=sp)
                return t.matmul(pB[32:33, 1, :], env["ones_bf"][:, :], x2bf[:, bf, 512:1024], start=st, stop=sp)
            w = [cast_done]
            if dc == 0:
                w += [self.tok["pB_free"]]
            if mm_done is None:
                mm_done = {}
            mm_done[dc] = self.op("tensor", mm, waits=w)
            xbf_user[bf] = mm_done[dc]

        c = 1.0 / D

        def r1(v):
            v.tensor_scalar(out=rows[:, 0:512], in0=pB[0:1, 0, :], scalar1=c, scalar2=0.0, op0=ALU.mult, op1=ALU.add)
            v.tensor_scalar(out=rows[:, 512:1024], in0=pB[0:1, 1, :], scalar1=c, scalar2=0.0, op0=ALU.mult, op1=ALU.add)
            v.tensor_scalar(out=rows2[:, 0:512], in0=pB[32:33, 0, :], scalar1=c, scalar2=0.0, op0=ALU.mult, op1=ALU.add)
            return v.tensor_scalar(out=rows2[:, 512:1024], in0=pB[32:33, 1, :], scalar1=c, scalar2=0.0, op0=ALU.mult, op1=ALU.add)
        t = self.op("vector", r1, waits=[mm_done[DC - 1]])
        self.tok["pB_free"] = t

        t = self.op("vector", lambda v: v.tensor_tensor(out=rows3[:, :], in0=rows[:, :], in1=rows[:, :], op=ALU.mult), waits=[t])
        t = self.op("vector", lambda v: v.tensor_tensor(out=rows2[:, :], in0=rows2[:, :], in1=rows3[:, :], op=ALU.subtract), waits=[t])
        t = self.op("vector", lambda v: v.tensor_scalar(out=rows2[:, :], in0=rows2[:, :], scalar1=EPS, scalar2=0.0, op0=ALU.add, op1=ALU.add), waits=[t])
        t = self.op("act", lambda a: a.activation(out=rows3[:, :], in_=rows2[:, :], func=AF.Sqrt), waits=[t])
        t = self.op("vector", lambda v: v.reciprocal(out=rows3[:, :], in_=rows3[:, :]), waits=[t])
        t = self.op("vector", lambda v: v.tensor_tensor(out=rows4, in0=rows3[:, :], in1=rows3[:, :], op=ALU.mult), waits=[t])
        t = self.op("vector", lambda v: v.tensor_tensor(out=rows4, in0=rows4, in1=rows2[:, :], op=ALU.mult), waits=[t])
        t = self.op("vector", lambda v: v.tensor_scalar(out=rows4, in0=rows4, scalar1=-0.5, scalar2=1.5, op0=ALU.mult, op1=ALU.add), waits=[t])
        t_rstd = self.op("vector", lambda v: v.tensor_tensor(out=rows3[:, :], in0=rows3[:, :], in1=rows4[:, :], op=ALU.mult), waits=[t])

        xn = env["xn"]
        norm_done = {}
        tmpf_user = {0: self.tok["tmpf_free"], 1: self.tok["tmpf_free"]}
        prev_half = None
        for h in (0, 1):
            def bc(t_, h=h):
                t_.matmul(pC[:, 0, :], env["ones_row"][:, :], rows[:, h * 512:(h + 1) * 512], start=True, stop=True)
                return t_.matmul(pC[:, 1, :], env["ones_row"][:, :], rows3[:, h * 512:(h + 1) * 512], start=True, stop=True)
            w = [t_rstd, self.tok["pC_free"]]
            if prev_half is not None:
                w.append(prev_half)
            t_bc = self.op("tensor", bc, waits=w)

            for dc in range(DC):
                bf = dc % 2
                t_ld = self.op("sync", lambda sy, dc=dc, bf=bf, h=h: sy.dma_start(
                    xch[:, bf, h * 512:(h + 1) * 512], src[dc, :, h * 512:(h + 1) * 512]),
                    waits=[xch_user[bf], self.tok["x_src_chunks"].get((dc, h), V0)],
                    by=16, sem="sXa" if bf == 0 else "sXb")
                t_n1 = self.op("vector", lambda v, bf=bf, h=h: v.tensor_tensor(
                    out=env["tmpf"][:, bf, :], in0=xch[:, bf, h * 512:(h + 1) * 512], in1=pC[:, 0, :], op=ALU.subtract),
                    waits=[t_ld, t_bc, tmpf_user[bf]])
                t_n2 = self.op("vector", lambda v, dc=dc, bf=bf, h=h: v.tensor_tensor(
                    out=xn[:, dc, h * 512:(h + 1) * 512], in0=env["tmpf"][:, bf, :], in1=pC[:, 1, :], op=ALU.mult),
                    waits=[t_n1, self.tok["xn_free"] if (h == 0 and dc == 0) else T0])
                norm_done[(h, dc)] = t_n2
                xch_user[bf] = t_n1
                tmpf_user[bf] = t_n2
            prev_half = norm_done[(h, DC - 1)]
        self.tok["pC_free"] = norm_done[(1, DC - 1)]
        self.tok["tmpf_free"] = norm_done[(1, DC - 1)]
        self.tok["x2bf_free"] = mm_done[DC - 1]
        return (norm_done[(0, DC - 1)], norm_done[(1, DC - 1)])

    def ffn_sublayer(self, src, dst, next_src=None):
        env = self.env
        xn = env["xn"]
        pA, pB, pC = env["pA"], env["pB"], env["pC"]

        xn_h = self.ln_pass(src)
        xn_ok = xn_h[1]

        def rmm(t, h):
            r = None
            for dc in range(DC):
                r = t.matmul(pB[0:NEXP, h, :], env["rw_sb"][:, dc, :],
                             xn[:, dc, h * 512:(h + 1) * 512], start=(dc == 0), stop=(dc == DC - 1))
            return r
        t_rmm = self.op("tensor", lambda t: rmm(t, 0),
                        waits=[xn_h[0], self.tok["small_w"], self.tok["pB_free"]])
        t_rmm = self.op("tensor", lambda t: rmm(t, 1), waits=[xn_h[1]])

        def revac(v):
            v.tensor_scalar(out=env["logit_sb"][:, 0:512], in0=pB[0:NEXP, 0, :],
                            scalar1=env["rbias_sb"][:, :], scalar2=1.0, op0=ALU.add, op1=ALU.mult)
            return v.tensor_scalar(out=env["logit_sb"][:, 512:1024], in0=pB[0:NEXP, 1, :],
                                   scalar1=env["rbias_sb"][:, :], scalar2=1.0, op0=ALU.add, op1=ALU.mult)
        t_lg = self.op("vector", revac, waits=[t_rmm, self.tok["pro_v"], self.tok["oneh"]])
        self.tok["pB_free"] = t_lg

        def ltr(t):
            r = None
            for rt in range(NROW):
                r = t.transpose(pC[:, 0, rt * 64:rt * 64 + NEXP],
                                env["logit_sb"][0:NEXP, rt * 128:(rt + 1) * 128],
                                env["ident_sb"][0:NEXP, 0:NEXP])
            return r
        t_ltr = self.op("tensor", ltr, waits=[t_lg, self.tok["pC_free"], self.tok["consts"]])

        probs, racc, spw = env["probs"], env["racc"], env["spw"]
        rmax, rsum = env["rmax"], env["rsum"]

        def smax1(v):
            r = None
            for rt in range(NROW):
                r = v.tensor_scalar(out=probs[:, rt, :], in0=pC[:, 0, rt * 64:rt * 64 + NEXP],
                                    scalar1=1.0, scalar2=-3.0e38, op0=ALU.mult, op1=ALU.max,
                                    accum_out=rmax[:, rt:rt + 1])
            return r
        t = self.op("vector", smax1, waits=[t_ltr])
        self.tok["pC_free"] = t
        t = self.op("vector", lambda v: v.tensor_scalar(
            out=rmax[:, :], in0=rmax[:, :], scalar1=-1.0, scalar2=0.0, op0=ALU.mult, op1=ALU.add), waits=[t])

        def smax3(a):
            r = None
            for rt in range(NROW):
                r = a.activation(out=probs[:, rt, :], in_=probs[:, rt, :], func=AF.Exp,
                                 bias=rmax[:, rt:rt + 1], accum_out=rsum[:, rt:rt + 1])
            return r
        t = self.op("act", smax3, waits=[t])
        t = self.op("vector", lambda v: v.reciprocal(out=rsum[:, :], in_=rsum[:, :]), waits=[t])

        def smax5(v):
            for rt in range(NROW):
                v.tensor_scalar(out=probs[:, rt, :], in0=probs[:, rt, :],
                                scalar1=rsum[:, rt:rt + 1], scalar2=0.0, op0=ALU.mult, op1=ALU.add)
            v.memset(racc[:, :, :], 0.0)
            return v.memset(env["racc2"][:, :, :], 0.0)
        t = self.op("vector", smax5, waits=[t])

        ta = tb = t
        for m in range(NEXP):
            acc = racc if m % 2 == 0 else env["racc2"]

            def cnt(v, m=m, acc=acc):
                r = None
                for rt in range(NROW):
                    r = v.scalar_tensor_tensor(out=acc[:, rt, :], in0=probs[:, rt, :],
                                               scalar=probs[:, rt, m:m + 1], in1=acc[:, rt, :],
                                               op0=ALU.is_lt, op1=ALU.add)
                return r
            if m % 2 == 0:
                ta = self.op("vector", cnt, waits=[ta])
            else:
                tb = self.op("vector", cnt, waits=[tb])

        def cmerge(v):
            r = None
            for rt in range(NROW):
                r = v.tensor_tensor(out=racc[:, rt, :], in0=racc[:, rt, :],
                                    in1=env["racc2"][:, rt, :], op=ALU.add)
            return r
        t = self.op("vector", cmerge, waits=[ta, tb])

        def selw(v):
            r = None
            for rt in range(NROW):
                r = v.scalar_tensor_tensor(out=spw[:, rt, :], in0=racc[:, rt, :],
                                           scalar=8.0, in1=probs[:, rt, :],
                                           op0=ALU.is_lt, op1=ALU.mult)
            return r
        t_spw = self.op("vector", selw, waits=[t])

        def str_(t):
            r = None
            for rt in range(NROW):
                r = t.transpose(pC[0:NEXP, rt // 4, (rt % 4) * 128:(rt % 4 + 1) * 128],
                                spw[:, rt, :], env["ident_sb"][:, :])
            return r
        t_str = self.op("tensor", str_, waits=[t_spw, self.tok["pC_free"]])

        def spwt(v):
            v.tensor_copy(env["spw_t"][:, 0:512], pC[0:NEXP, 0, :])
            return v.tensor_copy(env["spw_t"][:, 512:1024], pC[0:NEXP, 1, :])
        t_spwt = self.op("vector", spwt, waits=[t_str])
        self.tok["pC_free"] = t_spwt

        def wmm(t):
            t.matmul(pB[0:E, 0, :], env["m_onehot"][:, :], env["spw_t"][:, 0:512], start=True, stop=True)
            return t.matmul(pB[0:E, 1, :], env["m_onehot"][:, :], env["spw_t"][:, 512:1024], start=True, stop=True)
        t_wmm = self.op("tensor", wmm, waits=[t_spwt, self.tok["pB_free"]])

        def wevac(v):
            v.tensor_copy(env["wt_sb"][:, 0:512], pB[0:E, 0, :])
            return v.tensor_copy(env["wt_sb"][:, 512:1024], pB[0:E, 1, :])
        t_wt = self.op("vector", wevac, waits=[t_wmm])
        self.tok["pB_free"] = t_wt

        def wrepmm(t):
            t.matmul(pB[:, 0, :], env["e16_sb"][:, :], env["wt_sb"][:, 0:512], start=True, stop=True)
            return t.matmul(pB[:, 1, :], env["e16_sb"][:, :], env["wt_sb"][:, 512:1024], start=True, stop=True)
        t_wrep = self.op("tensor", wrepmm, waits=[t_wt])

        def wrepev(v):
            v.tensor_copy(env["wrep_sb"][:, 0:512], pB[:, 0, :])
            return v.tensor_copy(env["wrep_sb"][:, 512:1024], pB[:, 1, :])
        t_wrepev = self.op("vector", wrepev, waits=[t_wrep])
        self.tok["pB_free"] = t_wrepev

        def hmm(t):
            r = None
            for h in (0, 1):
                for dc in range(DC):
                    r = t.matmul(pC[:, h, :], env["a_sb"][:, dc, :], xn[:, dc, h * 512:(h + 1) * 512],
                                 start=(dc == 0), stop=(dc == DC - 1))
            return r
        t_hmm = self.op("tensor", hmm, waits=[xn_ok, self.tok["pC_free"]])
        del xn_ok

        def hwmul(v):
            v.tensor_tensor(out=env["hw_sb"][:, 0:512], in0=pC[:, 0, :], in1=env["wrep_sb"][:, 0:512], op=ALU.mult)
            return v.tensor_tensor(out=env["hw_sb"][:, 512:1024], in0=pC[:, 1, :], in1=env["wrep_sb"][:, 512:1024], op=ALU.mult)
        t_hw = self.op("vector", hwmul, waits=[t_hmm, t_wrepev])
        self.tok["pC_free"] = t_hw

        wslab, x1b, tmpf, resch = env["wslab"], env["x1buf"], env["tmpf"], env["resch"]
        slab_mm = {}
        evd = {}
        std = {}
        for oc in range(DC):
            wb = oc % 2
            t_w = self.op("gpsimd", lambda g, oc=oc, wb=wb: g.dma_start(
                wslab[:, wb, :, :], env["bw_s"][oc, :, :, :]),
                waits=[slab_mm.get(oc - 2, self.tok["wslab_free"])], by=16,
                sem="sWa" if wb == 0 else "sWb")

            for h in (0, 1):
                ob = h

                def mms(t, oc=oc, h=h, wb=wb, ob=ob):
                    for dc in range(DC):
                        t.matmul(pA[:, ob, :], wslab[:, wb, dc, :], xn[:, dc, h * 512:(h + 1) * 512],
                                 start=(dc == 0), stop=False)
                    return t.matmul(pA[:, ob, :], env["b_sb"][:, oc, :], env["hw_sb"][:, h * 512:(h + 1) * 512],
                                    start=False, stop=True)
                w = [t_w, evd.get((oc - 1, h), self.tok["pA_free"])]
                if oc == 0:
                    w += [xn_h[h], t_hw, self.tok["small_w"]]
                t_mm = self.op("tensor", mms, waits=w)
                slab_mm[oc] = t_mm

                t_r = self.op("sync", lambda sy, oc=oc, h=h, ob=ob: sy.dma_start(
                    resch[:, ob, :], src[oc, :, h * 512:(h + 1) * 512]),
                    waits=[evd.get((oc - 1, h), self.tok["resch_free"]),
                           self.tok["x_src_chunks"].get((oc, h), V0)],
                    by=16, sem="sRa" if ob == 0 else "sRb")

                t_e1 = self.op("vector", lambda v, oc=oc, ob=ob: v.tensor_scalar(
                    out=tmpf[:, ob, :], in0=pA[:, ob, :],
                    scalar1=0.5, scalar2=env["halfb"][:, oc:oc + 1], op0=ALU.mult, op1=ALU.add),
                    waits=[t_mm, self.tok["pro_v"],
                           evd.get((oc - 1, h), self.tok["tmpf_free"])])
                t_e2 = self.op("vector", lambda v, ob=ob: v.tensor_tensor(
                    out=x1b[:, ob, :], in0=tmpf[:, ob, :], in1=resch[:, ob, :], op=ALU.add),
                    waits=[t_e1, t_r, std.get((oc - 1, h), self.tok["x1b_free"][h])])
                evd[(oc, h)] = t_e2

                t_s = self.op("act", lambda sc, oc=oc, h=h, ob=ob: sc.dma_start(
                    dst[oc, :, h * 512:(h + 1) * 512], x1b[:, ob, :]),
                    waits=[t_e2], by=16, sem="sOa" if ob == 0 else "sOb")
                std[(oc, h)] = t_s
            if next_src is not None:
                if oc >= 4:
                    self.emit_stats_mm(oc - 4)
                if oc >= 2:
                    self.emit_stats_load(next_src, oc - 2, std)
        if next_src is not None:
            for dcx in (DC - 2, DC - 1):
                self.emit_stats_mm(dcx - 2)
                self.emit_stats_load(next_src, dcx, std)
            for dcx in (DC - 2, DC - 1):
                self.emit_stats_mm(dcx)

        self.tok["pA_free"] = evd[(DC - 1, 1)]
        self.tok["xn_free"] = slab_mm[DC - 1]
        self.tok["wslab_free"] = slab_mm[DC - 1]
        self.tok["resch_free"] = evd[(DC - 1, 1)]
        self.tok["tmpf_free"] = evd[(DC - 1, 1)]
        self.tok["x1b_free"] = {0: std[(DC - 1, 0)], 1: std[(DC - 1, 1)]}
        self.tok["last_store"] = [std[(DC - 1, 0)], std[(DC - 1, 1)]]
        self.tok["x_src"] = [std[(DC - 1, 0)], std[(DC - 1, 1)]]
        self.tok["x_src_chunks"] = dict(std)

    def attn_sublayer(self, dst, next_src=None):
        env = self.env
        xn = env["xn"]
        pA, pB, pC = env["pA"], env["pB"], env["pC"]
        wslab = env["wslab"]
        x_res = env["x_res"]

        xn_h = self.ln_pass(x_res)

        kv_store = {}
        slab_mm = {}
        evp = {}
        vtst_free = O0
        vtr_done = {}
        for idx in range(2 * NKV):
            isv = idx >= NKV
            g = idx % NKV
            wb = idx % 2
            wsrc = env["wv_s"] if isv else env["wk_s"]
            t_w = self.op("gpsimd", lambda gp, g=g, wb=wb, wsrc=wsrc: gp.dma_start(
                wslab[:, wb, :, :], wsrc[g, :, :, :]),
                waits=[slab_mm.get(idx - 2, self.tok["wslab_free"])], by=16,
                sem="sWa" if wb == 0 else "sWb")

            mmt = None
            for h in (0, 1):
                ob = h

                def mms(t, h=h, wb=wb, ob=ob):
                    r = None
                    for dc in range(DC):
                        r = t.matmul(pA[:, ob, :], wslab[:, wb, dc, :], xn[:, dc, h * 512:(h + 1) * 512],
                                     start=(dc == 0), stop=(dc == DC - 1))
                    return r
                w = [t_w, evp.get((idx - 1, h), self.tok["pA_free"])]
                if idx == 0:
                    w += [xn_h[h]]
                mmt = self.op("tensor", mms, waits=w)

                t_e = self.op("vector", lambda v, h=h, ob=ob, isv=isv, wb=wb: v.tensor_copy(
                    (env["vstage"] if isv else env["kstage"])[:, h * 512:(h + 1) * 512], pA[:, ob, :]),
                    waits=[mmt, evp.get((idx - 1, h), V0),
                           kv_store.get((idx - 1, h), O0),
                           vtr_done.get(idx - 1, V0),
                           self.tok.get("k2_%d" % (g - 1), V0) if not isv else V0])
                evp[(idx, h)] = t_e
            slab_mm[idx] = mmt

            if not isv:
                t_s = self.op("act", lambda sc, g=g: sc.dma_start(
                    env["kv_cont"][g, :, :], env["kstage"][:, :]),
                    waits=[evp[(idx, 0)], evp[(idx, 1)]], by=16, sem="sOK")
                kv_store[(idx, 0)] = kv_store[(idx, 1)] = t_s
                def flush_kn(g2):
                    def knmm(t):
                        t.matmul(pB[0:1, 0, :], env["ones_bf"][:, :], env["nsq"][:, 0:512], start=True, stop=True)
                        return t.matmul(pB[0:1, 1, :], env["ones_bf"][:, :], env["nsq"][:, 512:1024], start=True, stop=True)
                    t_kn = self.op("tensor", knmm,
                                   waits=[self.tok["k2_%d" % g2],
                                          self.tok.get("knred_%d" % (g2 - 1),
                                                       self.tok["pB_free"] if g2 == 0 else T0)])
                    self.tok["knmm_%d" % g2] = t_kn

                    def knred(v):
                        v.tensor_reduce(out=env["scr2"][0:1, 8:9], in_=pB[0:1, 0, :], axis=AX.X, op=ALU.max)
                        return v.tensor_reduce(out=env["scr2"][0:1, 9:10], in_=pB[0:1, 1, :], axis=AX.X, op=ALU.max)
                    t_kr = self.op("vector", knred, waits=[t_kn, self.tok.get("knred_%d" % (g2 - 1), V0)])
                    t_kr2 = self.op("vector", lambda v: v.tensor_reduce(
                        out=env["scr2"][0:1, 10:11], in_=env["scr2"][0:1, 8:10], axis=AX.X, op=ALU.max),
                        waits=[t_kr])
                    t_kr3 = self.op("vector", lambda v: v.tensor_tensor(
                        out=env["kn2max"][:, :], in0=env["kn2max"][:, :], in1=env["scr2"][0:1, 10:11], op=ALU.max),
                        waits=[t_kr2, self.tok["norminit"]])
                    self.tok["knred_%d" % g2] = t_kr3
                    return t_kr3

                if g > 0:
                    flush_kn(g - 1)
                t_k2 = self.op("vector", lambda v: v.tensor_tensor(
                    out=env["nsq"], in0=env["kstage"][:, :], in1=env["kstage"][:, :], op=ALU.mult),
                    waits=[evp[(idx, 0)], evp[(idx, 1)], self.tok.get("knmm_%d" % (g - 1), T0)])
                self.tok["k2_%d" % g] = t_k2
                if g == NKV - 1:
                    t_kr3 = flush_kn(g)
                    t_z = self.op("vector", lambda v: v.memset(env["vtstage"][:, :, :], 0.0),
                                  waits=[t_kr3, vtst_free])
                    t_knc = self.op("vector", lambda v: v.tensor_copy(
                        env["vtstage"][0:1, 0, 0:1], env["kn2max"][:, :]), waits=[t_z])
                    t_kns = self.op("act", lambda sc: sc.dma_start(
                        env["kv_cont"][16, :, :], env["vtstage"][:, :, :].rearrange("p a b -> p (a b)")),
                        waits=[t_knc, t_s], by=16, sem="sOK")
                    kv_store[(idx, 0)] = kv_store[(idx, 1)] = t_kns
                    self.tok["pB_free"] = t_kr3
                    self.tok["vt_zeroed"] = t_kns
            else:
                prev_tt = {}
                last_ev = None
                for tt in range(NROW):
                    def vtr1(t, tt=tt):
                        return t.transpose(env["pV"][:, tt % 2, 0:128], env["vstage"][:, tt * 128:(tt + 1) * 128],
                                           env["identb_sb"][:, :])
                    w = [evp[(idx, 0)], evp[(idx, 1)], self.tok["consts"]]
                    if tt < 2:
                        w.append(self.tok.get("pV_free", V0))
                    else:
                        w.append(prev_tt[tt - 2])
                    t_tr = self.op("tensor", vtr1, waits=w)
                    t_ev = self.op("vector", lambda v, tt=tt: v.tensor_copy(
                        env["vtstage"][:, tt, :], env["pV"][:, tt % 2, 0:128]),
                        waits=[t_tr, vtst_free, self.tok.get("vt_zeroed", O0)])
                    prev_tt[tt] = t_ev
                    last_ev = t_ev
                vtr_done[idx] = last_ev
                self.tok["pV_free"] = last_ev

                def vst(sc, s, g=g):
                    for tt in range(NROW):
                        sc.dma_start(env["kv_cont"][NKV + tt, :, g * 128:(g + 1) * 128],
                                     env["vtstage"][:, tt, :]).then_inc(s, 16)
                t_s = self.multi("act", vst, waits=[last_ev], n=NROW, sem="sOV")
                vtst_free = t_s
                kv_store[(idx, 0)] = kv_store[(idx, 1)] = t_s
        kv_all_stored = kv_store[(2 * NKV - 1, 0)]
        self.tok["pA_free"] = evp[(2 * NKV - 1, 1)]

        t_cc = self.op("gpsimd", lambda g: g.collective_compute(
            "AllGather", ALU.bypass,
            replica_groups=[[0, 1, 2, 3], [4, 5, 6, 7]],
            ins=[env["kv_cont"][:, :, :]],
            outs=[env["kv_gath"][:, :, :, :]]),
            waits=[kv_all_stored], sem="sC")

        owner = {}
        for rr in range(4):
            for ss, gg in enumerate(rows_for_rank(rr)):
                owner[gg] = (rr, ss)

        def reorder(g, s):
            for j in range(32):
                rr, ss = owner[j]
                g.dma_start(env["k_seq"][:, :, j * 128:(j + 1) * 128],
                            env["kv_gath"][rr, 0:NKV, :, ss * 128:(ss + 1) * 128]).then_inc(s, 16)
                g.dma_start(env["v_seq"][j, :, :],
                            env["kv_gath"][rr, NKV + ss, :, :]).then_inc(s, 16)
        t_reord = self.multi("sync", reorder, waits=[t_cc], n=64, sem="sRE")

        qs_store = {}
        qevd = {}
        q2_done = {}
        qn_mm = {}
        qn_red = {}
        for qh in range(NH):
            idx = 2 * NKV + qh
            wb = idx % 2
            t_w = self.op("gpsimd", lambda gp, qh=qh, wb=wb: gp.dma_start(
                wslab[:, wb, :, :], env["wq_s"][qh, :, :, :]),
                waits=[slab_mm.get(idx - 2, self.tok["wslab_free"])], by=16,
                sem="sWa" if wb == 0 else "sWb")

            mmt = None
            for h in (0, 1):
                ob = h

                def mms(t, h=h, wb=wb, ob=ob):
                    r = None
                    for dc in range(DC):
                        r = t.matmul(pA[:, ob, :], wslab[:, wb, dc, :], xn[:, dc, h * 512:(h + 1) * 512],
                                     start=(dc == 0), stop=(dc == DC - 1))
                    return r
                w = [t_w, qevd.get((qh - 1, h), evp.get((2 * NKV - 1, h), V0))]
                if qh == 0:
                    w.append(self.tok["pA_free"])
                mmt = self.op("tensor", mms, waits=w)
                t_e = self.op("vector", lambda v, h=h, ob=ob, qh=qh: v.tensor_copy(
                    env["qstage"][:, qh % 2, h * 512:(h + 1) * 512], pA[:, ob, :]),
                    waits=[mmt, qs_store.get(qh - 2, O0), q2_done.get(qh - 2, V0)])
                qevd[(qh, h)] = t_e
            slab_mm[idx] = mmt

            t_s = self.op("act", lambda sc, qh=qh: sc.dma_start(
                env["q_dram"][qh, :, :], env["qstage"][:, qh % 2, :]),
                waits=[qevd[(qh, 0)], qevd[(qh, 1)]], by=16,
                sem="sQa" if qh % 2 == 0 else "sQb")
            qs_store[qh] = t_s

            def flush_qn(qh2):
                def qnmm(t):
                    t.matmul(pB[0:1, 0, :], env["ones_bf"][:, :], env["hw_sb"][:, 0:512], start=True, stop=True)
                    return t.matmul(pB[0:1, 1, :], env["ones_bf"][:, :], env["hw_sb"][:, 512:1024], start=True, stop=True)
                t_qn = self.op("tensor", qnmm, waits=[q2_done[qh2], qn_red.get(qh2 - 1, self.tok["pB_free"])])
                qn_mm[qh2] = t_qn

                def qred(v):
                    v.tensor_reduce(out=env["scr2"][0:1, 0:1], in_=pB[0:1, 0, :], axis=AX.X, op=ALU.max)
                    return v.tensor_reduce(out=env["scr2"][0:1, 1:2], in_=pB[0:1, 1, :], axis=AX.X, op=ALU.max)
                t_qr = self.op("vector", qred, waits=[t_qn, self.tok["norminit"], qn_red.get(qh2 - 1, V0)])
                t_qr2 = self.op("vector", lambda v: v.tensor_reduce(
                    out=env["scr2"][0:1, 2:3], in_=env["scr2"][0:1, 0:2], axis=AX.X, op=ALU.max), waits=[t_qr])
                t_qr3 = self.op("vector", lambda v: v.tensor_tensor(
                    out=env["qn2max"][:, :], in0=env["qn2max"][:, :], in1=env["scr2"][0:1, 2:3], op=ALU.max), waits=[t_qr2])
                qn_red[qh2] = t_qr3

            if qh > 0:
                flush_qn(qh - 1)
            t_q2 = self.op("vector", lambda v, qh=qh: v.tensor_tensor(
                out=env["nsq"], in0=env["qstage"][:, qh % 2, :], in1=env["qstage"][:, qh % 2, :], op=ALU.mult),
                waits=[qevd[(qh, 0)], qevd[(qh, 1)], qn_mm.get(qh - 1, self.tok["xn_free"])])
            q2_done[qh] = t_q2
            if qh == NH - 1:
                flush_qn(qh)
        self.tok["pB_free"] = qn_red[NH - 1]
        self.tok["pA_free"] = qevd[(NH - 1, 1)]
        self.tok["xn_free"] = slab_mm[2 * NKV + NH - 1]
        self.tok["wslab_free"] = slab_mm[2 * NKV + NH - 1]
        qn_done = qn_red[NH - 1]
        qproj_mm_last = slab_mm[2 * NKV + NH - 1]

        t_knl = self.op("sync", lambda sy: sy.dma_start(
            env["knstage"][0:1, 0:4],
            env["kv_gath"][0:4, 16:17, 0:1, 0:1].rearrange("r a p t -> (a p) (r t)"),
            allow_slow_non_contiguous=True),
            waits=[t_cc, ("sOK", self.cnt.get("sOK", 0))], by=16, sem="sKN")
        t_kn1 = self.op("vector", lambda v: v.tensor_copy(env["scr2"][0:1, 11:15], env["knstage"][0:1, 0:4]),
                        waits=[t_knl])
        kn_done = self.op("vector", lambda v: v.tensor_reduce(
            out=env["kn2max"][:, :], in_=env["scr2"][0:1, 11:15], axis=AX.X, op=ALU.max),
            waits=[t_kn1])

        t = self.op("vector", lambda v: v.tensor_tensor(
            out=env["scr2"][0:1, 4:5], in0=env["qn2max"][:, :], in1=env["kn2max"][:, :], op=ALU.mult),
            waits=[qn_done, kn_done])
        t = self.op("act", lambda a: a.activation(out=env["scr2"][0:1, 5:6], in_=env["scr2"][0:1, 4:5], func=AF.Sqrt), waits=[t])
        t = self.op("vector", lambda v: v.tensor_scalar(
            out=env["scr2"][0:1, 6:7], in0=env["scr2"][0:1, 5:6], scalar1=-SCALE, scalar2=0.0, op0=ALU.mult, op1=ALU.add), waits=[t])
        t_nm = self.op("tensor", lambda te: te.matmul(
            pC[:, 0, 0:1], env["ones_row"][:, :], env["scr2"][0:1, 6:7], start=True, stop=True),
            waits=[t, self.tok["pC_free"]])
        t_nmc = self.op("vector", lambda v: v.tensor_copy(env["negm_sb"][:, :], pC[:, 0, 0:1]), waits=[t_nm])
        self.tok["pC_free"] = t_nmc

        wslab_h = env["wslab"]
        pt2_h = env["pt2"]

        def ptv(kb):
            if kb < 16:
                return bass.AP(wslab_h, kb * 512, [[8192, 128], [1, 512]])
            return pt2_h[:, kb - 16, :]

        def ptv2(j):
            if 2 * j < 16:
                return bass.AP(wslab_h, 2 * j * 512, [[8192, 128], [1, 1024]])
            return bass.AP(pt2_h, (2 * j - 16) * 512, [[8192, 128], [1, 1024]])

        def stb(kb):
            return (pA if (kb % 4) < 2 else pB)[:, kb % 2, :]

        mask_used = V0
        sl_free = {("k", 0): T0, ("k", 1): T0, ("v", 0): T0, ("v", 1): T0,
                   ("q", 0): T0, ("q", 1): T0}
        gidx = 0
        bank_free = {0: self.tok["pA_free"], 1: self.tok["pA_free"],
                     2: kn_done, 3: kn_done}
        pc_reader = {0: t_nmc, 1: self.tok["pC_free"]}
        last_exp = A0
        attn_done = None
        pend_lav = []
        pend_fin = None
        pop_last = T0

        def flush_fin():
            nonlocal pend_fin, attn_done
            if pend_fin is None:
                return
            ctx = pend_fin
            pend_fin = None
            cur = ctx["cur"]
            def _recip(v, pl=ctx["pl"]):
                with self.nc.allow_low_precision(reason="1/l in bf16: 0.4% rel on the softmax scale, within tolerance"):
                    return v.reciprocal(out=env["linv_bf"], in_=pC[0:1, pl, :])
            t_linv = self.op("vector", _recip,
                             waits=[cur["t_l"], self.tok.get("lb_last", T0)])
            t_lb = self.op("tensor", lambda te, pl=ctx["pl"]: te.matmul(
                pC[:, pl, :], env["ones_rbf"][:, :], env["linv_bf"], start=True, stop=True),
                waits=[t_linv])
            self.tok["lb_last"] = t_lb
            t_lbc = self.op("vector", lambda v, pl=ctx["pl"]: v.tensor_copy(env["linvb_sb"], pC[:, pl, :]),
                            waits=[t_lb])
            pc_reader[ctx["pl"]] = t_lbc

            def evac(v, g=ctx["g"], i=ctx["i"], pav=ctx["pav"]):
                return v.tensor_tensor(out=xn[:, 4 * g:4 * g + 4, i * 128:(i + 1) * 128],
                                       in0=pC[:, pav, :], in1=env["linvb_sb"], op=ALU.mult)
            t_ev = self.op("vector", evac, waits=[cur["t_av"], t_lbc, qproj_mm_last])
            pc_reader[ctx["pav"]] = t_ev
            sl_free[("v", 0)] = cur["t_av"]
            attn_done = t_ev

        for i in range(NROW):
            NB = 4 * (i + 1)
            L = 512 * (i + 1)
            mw = [mask_used]
            if i == 0:
                mw += [q2_done[NH - 1], qs_store[NH - 1], qs_store[NH - 2]]
            t_mask = self.op("sync", lambda sy, i=i: sy.dma_start(
                env["masks_sb"], env["masks"][i, :, :, :]),
                waits=mw, by=16, sem="sMK")
            mask_last = None

            for g in range(NKV):
                pav = g % 2
                pl = 1 - pav
                par = gidx % 2
                gidx += 1
                t_kld = self.op("sync", lambda sy, g=g, L=L, par=par: sy.dma_start(
                    env["ksl"][:, par, 0:L], env["k_seq"][g, :, 0:L]),
                    waits=[sl_free[("k", par)], t_reord], by=16,
                    sem="sKL" if par == 0 else "sKLb")
                t_qld = self.op("sync", lambda sy, g=g, i=i, par=par: sy.dma_start(
                    bass.AP(env["qsl"], par * 512, [[1024, 128], [128, 4], [1, 128]]),
                    env["q_dram"][4 * g:4 * g + 4, :, i * 128:(i + 1) * 128].rearrange("h p t -> p h t")),
                    waits=[qs_store[4 * g + 3], qs_store[4 * g + 2], sl_free[("q", par)]],
                    by=16, sem="sQL" if par == 0 else "sQLb")

                pt_ready = {}
                t_st = None
                cur = {"t_l": None, "t_av": None, "t_vld": None}

                def emit_vld(g=g, NB=NB, cur=cur):
                    cur["t_vld"] = self.op("sync", lambda sy, g=g, NB=NB: sy.dma_start(
                        env["vsl"][:, 0:NB, :],
                        env["v_seq"][0:NB, :, g * 128:(g + 1) * 128].rearrange("b p v -> p b v")),
                        waits=[sl_free[("v", 0)], t_reord], by=16, sem="sVL")

                def emit_l(kb, NB=NB, pl=pl, pt_ready=pt_ready, cur=cur):
                    lw = [pt_ready[kb]]
                    if kb == 0:
                        lw.append(pc_reader[pl])
                    cur["t_l"] = self.op("tensor", lambda t, kb=kb, NB=NB, pl=pl: t.matmul(
                        pC[0:1, pl, :], env["ones_bf"][:, :], ptv(kb),
                        start=(kb == 0), stop=(kb == NB - 1), skip_group_check=True), waits=lw)

                def emit_av(kb, NB=NB, pav=pav, pt_ready=pt_ready, cur=cur):
                    aw = [pt_ready[kb]]
                    if kb == 0:
                        aw += [cur["t_vld"], pc_reader[pav]]
                    cur["t_av"] = self.op("tensor", lambda t, kb=kb, NB=NB, pav=pav: t.matmul(
                        pC[:, pav, :], env["vsl"][:, kb, :], ptv(kb),
                        start=(kb == 0), stop=(kb == NB - 1), skip_group_check=True), waits=aw)

                def emit_lav(kb, el=emit_l, ea=emit_av):
                    el(kb)
                    ea(kb)

                if not pend_lav:
                    flush_fin()
                    emit_vld()
                for kb in range(NB):
                    def stmm(t, kb=kb, par=par):
                        return t.matmul(stb(kb), env["ksl"][:, par, kb * 128:(kb + 1) * 128],
                                        bass.AP(env["qsl"], par * 512, [[1024, 128], [128, 4], [1, 128]]), start=True, stop=True,
                                        skip_group_check=True)
                    t_st = self.op("tensor", stmm, waits=[t_kld, t_qld, bank_free[kb % 4]])
                    if pend_lav:
                        pend_lav.pop(0)()
                        pop_last = pend_fin["cur"]["t_av"]
                        if not pend_lav:
                            flush_fin()
                            emit_vld()

                    if kb % 2 == 1:
                        j = kb // 2

                        def expf(a, j=j):
                            return a.activation(out=ptv2(j), in_=(pA if j % 2 == 0 else pB)[:, :, :],
                                                func=AF.Exp, bias=env["negm_sb"][:, :], scale=SCALE)
                        t_ex = self.op("act", expf, waits=[t_st, t_nmc, pop_last])
                        last_exp = t_ex
                        b0 = (2 * j) % 4
                        bank_free[b0] = t_ex
                        bank_free[b0 + 1] = t_ex
                        for kk in (2 * j, 2 * j + 1):
                            pt_ready[kk] = t_ex
                            if kk >= NB - 4:
                                jj = kk - (NB - 4)
                                t_mk = self.op("vector", lambda v, kk=kk, jj=jj: v.tensor_tensor(
                                    out=ptv(kk), in0=ptv(kk),
                                    in1=bass.AP(env["qstage"], jj * 512, [[2048, 128], [1, 512]]),
                                    op=ALU.mult),
                                    waits=[t_ex, t_mask])
                                mask_last = t_mk
                                pt_ready[kk] = t_mk
                        if j >= 4:
                            emit_l(2 * (j - 4))
                            emit_l(2 * (j - 4) + 1)
                            emit_av(2 * (j - 4))
                            emit_av(2 * (j - 4) + 1)
                for kbx in range(max(0, NB - 8), NB):
                    pend_lav.append(lambda kb=kbx, f=emit_lav: f(kb))
                pend_fin = {"g": g, "i": i, "pav": pav, "pl": pl, "cur": cur}

                sl_free[("k", par)] = t_st
                sl_free[("q", par)] = t_st
            mask_used = mask_last
        while pend_lav:
            pend_lav.pop(0)()
        flush_fin()

        self.tok["pB_free"] = last_exp
        self.tok["pC_free"] = attn_done
        self.tok["wslab_free"] = attn_done

        evd = {}
        std = {}
        slab_mm2 = {}
        for oc in range(DC):
            wb = oc % 2
            t_w = self.op("gpsimd", lambda g, oc=oc, wb=wb: g.dma_start(
                wslab[:, wb, :, :], env["wo_s"][oc, :, :, :]),
                waits=[slab_mm2.get(oc - 2, self.tok["wslab_free"])], by=16,
                sem="sWa" if wb == 0 else "sWb")
            for h in (0, 1):
                ob = h

                def mms(t, oc=oc, h=h, wb=wb, ob=ob):
                    r = None
                    for dc in range(DC):
                        r = t.matmul(pA[:, ob, :], wslab[:, wb, dc, :], xn[:, dc, h * 512:(h + 1) * 512],
                                     start=(dc == 0), stop=(dc == DC - 1))
                    return r
                w = [t_w, evd.get((oc - 1, h), V0)]
                if oc == 0:
                    w.append(attn_done)
                t_mm = self.op("tensor", mms, waits=w)
                slab_mm2[oc] = t_mm

                t_r = self.op("sync", lambda sy, oc=oc, h=h, ob=ob: sy.dma_start(
                    env["resch"][:, ob, :], x_res[oc, :, h * 512:(h + 1) * 512]),
                    waits=[evd.get((oc - 1, h), self.tok["resch_free"]),
                           self.tok["x_src_chunks"].get((oc, h), V0)],
                    by=16, sem="sRa" if ob == 0 else "sRb")

                t_e2 = self.op("vector", lambda v, ob=ob: v.tensor_tensor(
                    out=env["x1buf"][:, ob, :], in0=pA[:, ob, :], in1=env["resch"][:, ob, :], op=ALU.add),
                    waits=[t_mm, t_r, std.get((oc - 1, h), self.tok["x1b_free"][h]),
                           evd.get((oc - 1, h), V0)])
                evd[(oc, h)] = t_e2

                t_s = self.op("act", lambda sc, oc=oc, h=h, ob=ob: sc.dma_start(
                    dst[oc, :, h * 512:(h + 1) * 512], env["x1buf"][:, ob, :]),
                    waits=[t_e2], by=16, sem="sOa" if ob == 0 else "sOb")
                std[(oc, h)] = t_s
            if next_src is not None:
                if oc >= 4:
                    self.emit_stats_mm(oc - 4)
                if oc >= 2:
                    self.emit_stats_load(next_src, oc - 2, std)
        if next_src is not None:
            for dcx in (DC - 2, DC - 1):
                self.emit_stats_mm(dcx - 2)
                self.emit_stats_load(next_src, dcx, std)
            for dcx in (DC - 2, DC - 1):
                self.emit_stats_mm(dcx)

        self.tok["pA_free"] = evd[(DC - 1, 1)]
        self.tok["xn_free"] = slab_mm2[DC - 1]
        self.tok["wslab_free"] = slab_mm2[DC - 1]
        self.tok["resch_free"] = evd[(DC - 1, 1)]
        self.tok["x1b_free"] = {0: std[(DC - 1, 0)], 1: std[(DC - 1, 1)]}
        self.tok["last_store"] = [std[(DC - 1, 0)], std[(DC - 1, 1)]]
        self.tok["x_src"] = [std[(DC - 1, 0)], std[(DC - 1, 1)]]
        self.tok["x_src_chunks"] = dict(std)

    def emit(self):
        q = self.q
        blk = self.env["blk"]

        @blk.sync
        def _(eng):
            for f in q["sync"]:
                f(eng)

        @blk.scalar
        def _(eng):
            for f in q["act"]:
                f(eng)

        @blk.gpsimd
        def _(eng):
            for f in q["gpsimd"]:
                f(eng)

        @blk.tensor
        def _(eng):
            for f in q["tensor"]:
                f(eng)

        @blk.vector
        def _(eng):
            for f in q["vector"]:
                f(eng)


def build_program():
    nc = bass.Bass("TRN2", target_bir_lowering=False)
    env = {}

    def din(name, shape, dt=F32):
        env[name] = nc.dram_tensor(name, shape, dt, kind="ExternalInput").ap()

    din("x_in", [DC, 128, TOK])
    din("wq_s", [NH, 128, DC, 128])
    din("wk_s", [NKV, 128, DC, 128])
    din("wv_s", [NKV, 128, DC, 128])
    din("wo_s", [DC, 128, DC, 128])
    din("bw_s", [DC, 128, DC, 128])
    din("bb_s", [128, DC])
    din("rw_s", [128, DC, NEXP])
    din("rb_s", [NEXP, 1])
    din("a_s", [128, DC, 128])
    din("b2_s", [128, DC, 128])
    din("emap", [NEXP, 1], I32)
    din("iota8", [NEXP, E])
    din("e16", [E, 128])
    din("ident32", [128, 128])
    din("identb", [128, 128], BF16)
    din("masks", [NROW, 128, 4, 512], BF16)

    env["out_x"] = nc.dram_tensor("out_x", [DC, 128, TOK], F32, kind="ExternalOutput").ap()
    env["x_res"] = nc.dram_tensor("x_res", [DC, 128, TOK], F32).ap()
    env["q_dram"] = nc.dram_tensor("q_dram", [NH, 128, TOK], BF16).ap()
    env["kv_cont"] = nc.dram_tensor("kv_cont", [17, 128, TOK], BF16).ap()
    env["kv_gath"] = nc.dram_tensor("kv_gath", [4, 17, 128, TOK], BF16).ap()
    env["k_seq"] = nc.dram_tensor("k_seq", [NKV, 128, S], BF16).ap()
    env["v_seq"] = nc.dram_tensor("v_seq", [32, 128, NKV * 128], BF16).ap()

    sb = [
        ("xn", [128, DC, TOK], BF16),
        ("wslab", [128, 2, DC, 128], BF16),
        ("xchunk", [128, 2, TOK], F32),
        ("xbf", [128, 2, TOK], BF16),
        ("x2bf", [128, 2, TOK], BF16),
        ("tmpf", [128, 2, 512], F32),
        ("resch", [128, 2, 512], F32),
        ("x1buf", [128, 2, 512], F32),
        ("ones_bf", [128, 1], BF16),
        ("ones_row", [1, 128], F32),
        ("ones_rbf", [1, 128], BF16),
        ("ident_sb", [128, 128], F32),
        ("identb_sb", [128, 128], BF16),
        ("rows", [1, 1024], F32),
        ("rows2", [1, 1024], F32),
        ("rows3", [1, 1024], F32),
        ("halfb", [128, DC], F32),
        ("rbias_sb", [NEXP, 1], F32),
        ("emap_i", [NEXP, 1], I32),
        ("emap_f", [NEXP, 1], F32),
        ("iota_sb", [NEXP, E], F32),
        ("m_onehot", [NEXP, E], F32),
        ("e16_sb", [E, 128], F32),
        ("rw_sb", [128, DC, NEXP], BF16),
        ("a_sb", [128, DC, 128], BF16),
        ("b_sb", [128, DC, 128], BF16),
        ("probs", [128, NROW, NEXP], F32),
        ("racc", [128, NROW, NEXP], F32),
        ("racc2", [128, NROW, NEXP], F32),
        ("spw", [128, NROW, NEXP], F32),
        ("rmax", [128, NROW], F32),
        ("rsum", [128, NROW], F32),
        ("logit_sb", [NEXP, TOK], F32),
        ("spw_t", [NEXP, TOK], F32),
        ("wrep_sb", [128, TOK], F32),
        ("hw_sb", [128, TOK], BF16),
        ("kstage", [128, TOK], BF16),
        ("vstage", [128, TOK], BF16),
        ("vtstage", [128, NROW, 128], BF16),
        ("qstage", [128, 2, TOK], BF16),
        ("qn2max", [1, 1], F32),
        ("kn2max", [1, 1], F32),
        ("scr2", [1, 16], F32),
        ("negm_sb", [128, 1], F32),
        ("knstage", [1, 4], BF16),
        ("ksl", [128, 2, S], BF16),
        ("vsl", [128, 32, 128], BF16),
        ("qsl", [128, 2, 512], BF16),
        ("pt2", [128, 16, 512], BF16),
    ]
    ps = [("pA", [128, 2, 512], F32), ("pB", [128, 2, 512], F32), ("pC", [128, 2, 512], F32), ("pV", [128, 2, 1024], BF16)]
    sems = ["sT", "sV", "sA", "sC", "sXa", "sXb", "sRa", "sRb", "sMK", "sKL", "sKLb", "sVL", "sQL", "sQLb", "sKN", "sCN", "sOa", "sOb", "sOK", "sOV", "sQa", "sQb", "sWa", "sWb", "sRE", "sSW"]

    import contextlib
    with contextlib.ExitStack() as ctx:
        for name, shape, dt in sb:
            env[name] = ctx.enter_context(nc.sbuf_tensor(name, shape, dt))
        env["rows4"] = env["logit_sb"][0:1, :]
        env["linv_bf"] = bass.AP(env["vtstage"], 0, [[1024, 1], [1, 512]])
        env["wt_sb"] = env["spw_t"][0:E, :]
        env["masks_sb"] = bass.AP(env["qstage"], 0, [[2048, 128], [512, 4], [1, 512]])
        env["nsq"] = env["hw_sb"][:, :]
        env["linv_row"] = env["spw_t"][0:1, 0:512]
        env["linvb_sb"] = env["wrep_sb"][:, 0:512]
        for name, shape, pdt in ps:
            env[name] = ctx.enter_context(nc.psum_tensor(name, shape, pdt))
        for name in sems:
            env[name] = ctx.enter_context(nc.semaphore(name))
        env["blk"] = ctx.enter_context(nc.Block())
        p = P(nc, env)
        p.plan()
        p.emit()
    return nc



def _slab(w):
    din, dout = w.shape
    return np.ascontiguousarray(
        np.asarray(w, dtype=np.float32).reshape(din // 128, 128, dout // 128, 128).transpose(2, 1, 0, 3))


def _host_shared(wq, wk, wv, wo, base_W, base_b, router_W, router_b, A, Bm, emap):
    import ml_dtypes
    d = {}
    wq = np.asarray(wq, dtype=np.float32).reshape(D, NH, HD)
    d["wq_s"] = np.ascontiguousarray(
        wq.reshape(DC, 128, NH, HD).transpose(2, 1, 0, 3))
    wk = np.asarray(wk, dtype=np.float32).reshape(D, NKV, HD)
    d["wk_s"] = np.ascontiguousarray(wk.reshape(DC, 128, NKV, HD).transpose(2, 1, 0, 3))
    wv = np.asarray(wv, dtype=np.float32).reshape(D, NKV, HD)
    d["wv_s"] = np.ascontiguousarray(wv.reshape(DC, 128, NKV, HD).transpose(2, 1, 0, 3))
    d["wo_s"] = _slab(wo)
    bw = np.asarray(base_W, dtype=np.float32)
    d["bw_s"] = np.ascontiguousarray(bw.reshape(DC, 128, DC, 128).transpose(0, 3, 2, 1))
    d["bb_s"] = np.ascontiguousarray(np.asarray(base_b, dtype=np.float32).reshape(DC, 128).T)
    rw = np.asarray(router_W, dtype=np.float32)
    d["rw_s"] = np.ascontiguousarray(rw.T.reshape(DC, 128, NEXP).transpose(1, 0, 2))
    d["rb_s"] = np.ascontiguousarray(np.asarray(router_b, dtype=np.float32).reshape(NEXP, 1))
    Aa = np.asarray(A, dtype=np.float32)
    d["a_s"] = np.ascontiguousarray(Aa.reshape(E * R, D).T.reshape(DC, 128, E * R).transpose(1, 0, 2))
    Bb = np.asarray(Bm, dtype=np.float32)
    d["b2_s"] = np.ascontiguousarray(
        Bb.transpose(0, 2, 1).reshape(E * R, DC, 128))
    d["emap"] = np.ascontiguousarray(np.asarray(emap, dtype=np.int32).reshape(NEXP, 1))
    d["iota8"] = np.ascontiguousarray(np.tile(np.arange(E, dtype=np.float32), (NEXP, 1)))
    e16 = np.zeros((E, 128), dtype=np.float32)
    for e in range(E):
        e16[e, e * R:(e + 1) * R] = 2.0
    d["e16"] = e16
    d["ident32"] = np.eye(128, dtype=np.float32)
    d["identb"] = np.eye(128, dtype=ml_dtypes.bfloat16)
    return d


def _mask_for_rank(r):
    import ml_dtypes
    rows = rows_for_rank(r)
    m = np.zeros((NROW, 4, 128, 512), dtype=np.float32)
    for i, g in enumerate(rows):
        qpos = g * 128 + np.arange(128)
        for j in range(4):
            kpos = (4 * i + j) * 128 + np.arange(128)
            vis = (kpos[:, None] <= qpos[None, :]).astype(np.float32)
            m[i, j] = np.tile(vis, (1, 4))
    m = m.transpose(0, 2, 1, 3)
    return np.ascontiguousarray(m.astype(ml_dtypes.bfloat16))


_NC_CACHE = {}


def kernel(L0_attn_x, L0_attn_wq, L0_attn_wk, L0_attn_wv, L0_attn_wo,
           L0_ffn_base_W, L0_ffn_base_b, L0_ffn_router_W, L0_ffn_router_b,
           L0_ffn_A, L0_ffn_B, L0_ffn_expert_map):
    x = np.asarray(L0_attn_x, dtype=np.float32)
    shared = _host_shared(L0_attn_wq, L0_attn_wk, L0_attn_wv, L0_attn_wo,
                          L0_ffn_base_W, L0_ffn_base_b, L0_ffn_router_W,
                          L0_ffn_router_b, L0_ffn_A, L0_ffn_B, L0_ffn_expert_map)
    if "nc" not in _NC_CACHE:
        _NC_CACHE["nc"] = build_program()
    nc = _NC_CACHE["nc"]

    in_maps = []
    meta = []
    for c in range(8):
        b, r = c // 4, c % 4
        rows = rows_for_rank(r)
        meta.append((b, rows))
        xi = np.concatenate([x[b, g * 128:(g + 1) * 128, :] for g in rows], axis=0)
        xi = np.ascontiguousarray(xi.T.reshape(DC, 128, TOK))
        im = dict(shared)
        im["x_in"] = xi
        im["masks"] = _mask_for_rank(r)
        in_maps.append(im)

    res = run_bass_kernel_spmd(nc, in_maps, core_ids=list(range(8)))

    out = np.empty((B, S, D), dtype=np.float32)
    for c in range(8):
        b, rows = meta[c]
        oc = np.asarray(res.results[c]["out_x"])
        ocv = np.ascontiguousarray(oc.reshape(D, TOK).T)
        for i, g in enumerate(rows):
            out[b, g * 128:(g + 1) * 128, :] = ocv[i * 128:(i + 1) * 128, :]
    return out


# version 18 (final)
# speedup vs baseline: 1.0703x; 1.0196x over previous
import numpy as np
import concourse.bass as bass
import concourse.mybir as mybir
from concourse.bass_utils import run_bass_kernel_spmd

F32 = mybir.dt.float32
BF16 = mybir.dt.bfloat16
I32 = mybir.dt.int32
AF = mybir.ActivationFunctionType
ALU = mybir.AluOpType
AX = mybir.AxisListType

B, S, D = 2, 4096, 4096
NH, NKV, HD = 32, 8, 128
SCALE = HD ** -0.5
TOK = 1024
NROW = 8
DC = D // 128
E, R = 8, 16
NEXP = 15
EPS = 1e-5

N_SUB = 3

T0 = ("sT", 0)
V0 = ("sV", 0)
A0 = ("sA", 0)
O0 = ("sO", 0)


def rows_for_rank(r):
    return sorted({r, 7 - r, 8 + r, 15 - r, 16 + r, 23 - r, 24 + r, 31 - r})


class P:

    def __init__(self, nc, env):
        self.nc = nc
        self.env = env
        self.q = {k: [] for k in ("sync", "act", "gpsimd", "tensor", "vector")}
        self.cnt = {}
        self.tok = {}

    def op(self, eng, emit, waits=(), sem=None, by=1):
        if sem is None:
            sem = {"act": "sA", "tensor": "sT", "vector": "sV"}[eng]
        s = self.env[sem]
        self.cnt[sem] = self.cnt.get(sem, 0) + by
        tgt = self.cnt[sem]
        ws = tuple((self.env[ks], v) for (ks, v) in waits if v > 0)

        def f(e, emit=emit, ws=ws, s=s, by=by):
            for (sm, v) in ws:
                e.wait_ge(sm, v)
            r = emit(e)
            if r is not None and by > 0:
                r.then_inc(s, by)
        self.q[eng].append(f)
        return (sem, tgt)

    def multi(self, eng, emit, waits=(), sem=None, n=1, by=16):
        s = self.env[sem]
        ws = tuple((self.env[ks], v) for (ks, v) in waits if v > 0)

        def f(e, emit=emit, ws=ws, s=s):
            for (sm, v) in ws:
                e.wait_ge(sm, v)
            emit(e, s)
        self.q[eng].append(f)
        self.cnt[sem] = self.cnt.get(sem, 0) + n * by
        return (sem, self.cnt[sem])

    def plan(self):
        env = self.env
        self.prologue()
        self.tok["x_src"] = []
        self.tok["x_src_chunks"] = {}
        self.ffn_sublayer(src=env["x_in"], dst=env["x_res"] if N_SUB > 1 else env["out_x"],
                          next_src=env["x_res"] if N_SUB > 1 else None)
        if N_SUB >= 2:
            self.attn_sublayer(dst=env["x_res"] if N_SUB > 2 else env["out_x"],
                               next_src=env["x_res"] if N_SUB > 2 else None)
        if N_SUB >= 3:
            self.ffn_sublayer(src=env["x_res"], dst=env["out_x"])
        self.op("act", lambda a: a.nop(), waits=list(self.tok["last_store"]), by=0)

    def prologue(self):
        env = self.env
        t_ones = self.op("vector", lambda v: v.memset(env["ones_bf"][:, :], 1.0))
        t_ones = self.op("vector", lambda v: v.memset(env["ones_row"][:, :], 1.0))
        t_ones = self.op("vector", lambda v: v.memset(env["ones_rbf"][:, :], 1.0))
        t_ones = self.op("vector", lambda v: v.memset(env["qn2max"][:, :], 0.0))
        t_ones = self.op("vector", lambda v: v.memset(env["kn2max"][:, :], 0.0))
        self.tok["ones"] = t_ones
        self.tok["norminit"] = t_ones

        def consts(sy, s):
            sy.dma_start(env["ident_sb"][:, :], env["ident32"][:, :]).then_inc(s, 16)
            sy.dma_start(env["identb_sb"][:, :], env["identb"][:, :]).then_inc(s, 16)
            sy.dma_start(env["rbias_sb"][:, :], env["rb_s"][:, :]).then_inc(s, 16)
            sy.dma_start(env["iota_sb"][:, :], env["iota8"][:, :]).then_inc(s, 16)
            sy.dma_start(env["halfb"][:, :], env["bb_s"][:, :]).then_inc(s, 16)
            sy.dma_start(env["e16_sb"][:, :], env["e16"][:, :]).then_inc(s, 16)
            sy.dma_start(env["emap_i"][:, :], env["emap"][:, :]).then_inc(s, 16)
        t_consts = self.multi("sync", consts, n=7, sem="sCN")
        self.tok["consts"] = t_consts

        def gp(g, s):
            g.dma_start(env["rw_sb"][:, :, :], env["rw_s"][:, :, :]).then_inc(s, 16)
            g.dma_start(env["a_sb"][:, :, :], env["a_s"][:, :, :]).then_inc(s, 16)
            g.dma_start(env["b_sb"][:, :, :], env["b2_s"][:, :, :]).then_inc(s, 16)
        t_gp = self.multi("gpsimd", gp, n=3, sem="sSW")
        self.tok["small_w"] = t_gp

        t = self.op("vector", lambda v: v.tensor_copy(env["emap_f"][:, :], env["emap_i"][:, :]),
                    waits=[t_consts])
        t_oneh = self.op("vector", lambda v: v.tensor_scalar(
            out=env["m_onehot"][:, :], in0=env["iota_sb"][:, :],
            scalar1=env["emap_f"][:, :], scalar2=0.0, op0=ALU.is_equal, op1=ALU.add),
            waits=[t])
        t_hb = self.op("vector", lambda v: v.tensor_scalar(
            out=env["halfb"][:, :], in0=env["halfb"][:, :],
            scalar1=0.5, scalar2=0.0, op0=ALU.mult, op1=ALU.add),
            waits=[t_consts])
        self.tok["pro_v"] = t_hb
        self.tok["oneh"] = t_oneh
        for k in ("xn_free", "pA_free_T", "wslab_free"):
            self.tok[k] = T0
        for k in ("pA_free", "pB_free", "pC_free", "pD_free",
                  "resch_free", "tmpf_free"):
            self.tok[k] = V0
        self.tok["x1b_free"] = {0: V0, 1: V0}

    def emit_stats_load(self, src, dc, store_toks):
        env = self.env
        xch, xbf, x2bf = env["xchunk"], env["xbf"], env["x2bf"]
        bf = dc % 2
        xch_user = self.tok["xch_user"]
        xbf_user = self.tok["xbf_user"]
        pre = self.tok.setdefault("pre_stats", {})
        sq = self.tok.setdefault("pre_sq", {})
        t_ld = self.op("sync", lambda sy, dc=dc, bf=bf: sy.dma_start(
            xch[:, bf, :], src[dc, :, :]),
            waits=[xch_user[bf], store_toks.get((dc, 0), V0), store_toks.get((dc, 1), V0)],
            by=16, sem="sXa" if bf == 0 else "sXb")
        t_c1 = self.op("vector", lambda v, bf=bf: v.tensor_copy(xbf[:, bf, :], xch[:, bf, :]),
                       waits=[t_ld, self.tok["ones"], xbf_user[bf]])
        t_c2 = self.op("act", lambda a, bf=bf: a.activation(
            out=x2bf[:, bf, :], in_=xbf[:, bf, :], func=AF.Square),
            waits=[t_c1, pre.get(dc - 2, self.tok.get("x2bf_free", T0))])
        sq[dc] = t_c2
        xch_user[bf] = t_c1

    def emit_stats_mm(self, dc):
        env = self.env
        xbf, x2bf = env["xbf"], env["x2bf"]
        pB = env["pB"]
        bf = dc % 2
        pre = self.tok["pre_stats"]

        def mm(t, dc=dc, bf=bf):
            st, sp = dc == 0, dc == DC - 1
            t.matmul(pB[0:1, 0, :], env["ones_bf"][:, :], xbf[:, bf, 0:512], start=st, stop=sp, skip_group_check=True)
            t.matmul(pB[0:1, 1, :], env["ones_bf"][:, :], xbf[:, bf, 512:1024], start=st, stop=sp, skip_group_check=True)
            t.matmul(pB[32:33, 0, :], env["ones_bf"][:, :], x2bf[:, bf, 0:512], start=st, stop=sp, skip_group_check=True)
            return t.matmul(pB[32:33, 1, :], env["ones_bf"][:, :], x2bf[:, bf, 512:1024], start=st, stop=sp, skip_group_check=True)
        w = [self.tok["pre_sq"][dc]]
        if dc == 0:
            w.append(self.tok["pB_free"])
        pre[dc] = self.op("tensor", mm, waits=w)
        self.tok["xbf_user"][bf] = pre[dc]

    def ln_pass(self, src):
        env = self.env
        xch, xbf, x2bf = env["xchunk"], env["xbf"], env["x2bf"]
        pB, pC = env["pB"], env["pC"]
        rows, rows2, rows3, rows4 = env["rows"], env["rows2"], env["rows3"], env["rows4"]
        xch_user = self.tok.setdefault("xch_user", {0: V0, 1: V0})
        xbf_user = self.tok.setdefault("xbf_user", {0: T0, 1: T0})

        mm_done = self.tok.pop("pre_stats", None)
        self.tok.pop("pre_sq", None)
        for dc in (() if mm_done is not None else range(DC)):
            bf = dc % 2
            ck = self.tok["x_src_chunks"]
            t_ld = self.op("sync", lambda sy, dc=dc, bf=bf: sy.dma_start(
                xch[:, bf, :], src[dc, :, :]),
                waits=[xch_user[bf], ck.get((dc, 0), V0), ck.get((dc, 1), V0)], by=16,
                sem="sXa" if bf == 0 else "sXb")
            t_c1 = self.op("vector", lambda v, bf=bf: v.tensor_copy(xbf[:, bf, :], xch[:, bf, :]),
                           waits=[t_ld, self.tok["ones"], xbf_user[bf]])
            t_c2 = self.op("act", lambda a, bf=bf: a.activation(
                out=x2bf[:, bf, :], in_=xbf[:, bf, :], func=AF.Square),
                waits=[t_c1, (mm_done or {}).get(dc - 2, self.tok.get("x2bf_free", T0))])
            cast_done = t_c2
            xch_user[bf] = t_c1

            def mm(t, dc=dc, bf=bf):
                st, sp = dc == 0, dc == DC - 1
                t.matmul(pB[0:1, 0, :], env["ones_bf"][:, :], xbf[:, bf, 0:512], start=st, stop=sp)
                t.matmul(pB[0:1, 1, :], env["ones_bf"][:, :], xbf[:, bf, 512:1024], start=st, stop=sp)
                t.matmul(pB[32:33, 0, :], env["ones_bf"][:, :], x2bf[:, bf, 0:512], start=st, stop=sp)
                return t.matmul(pB[32:33, 1, :], env["ones_bf"][:, :], x2bf[:, bf, 512:1024], start=st, stop=sp)
            w = [cast_done]
            if dc == 0:
                w += [self.tok["pB_free"]]
            if mm_done is None:
                mm_done = {}
            mm_done[dc] = self.op("tensor", mm, waits=w)
            xbf_user[bf] = mm_done[dc]

        c = 1.0 / D

        def r1(v):
            v.tensor_scalar(out=rows[:, 0:512], in0=pB[0:1, 0, :], scalar1=c, scalar2=0.0, op0=ALU.mult, op1=ALU.add)
            v.tensor_scalar(out=rows[:, 512:1024], in0=pB[0:1, 1, :], scalar1=c, scalar2=0.0, op0=ALU.mult, op1=ALU.add)
            v.tensor_scalar(out=rows2[:, 0:512], in0=pB[32:33, 0, :], scalar1=c, scalar2=0.0, op0=ALU.mult, op1=ALU.add)
            return v.tensor_scalar(out=rows2[:, 512:1024], in0=pB[32:33, 1, :], scalar1=c, scalar2=0.0, op0=ALU.mult, op1=ALU.add)
        t = self.op("vector", r1, waits=[mm_done[DC - 1]])
        self.tok["pB_free"] = t

        t = self.op("vector", lambda v: v.tensor_tensor(out=rows3[:, :], in0=rows[:, :], in1=rows[:, :], op=ALU.mult), waits=[t])
        t = self.op("vector", lambda v: v.tensor_tensor(out=rows2[:, :], in0=rows2[:, :], in1=rows3[:, :], op=ALU.subtract), waits=[t])
        t = self.op("vector", lambda v: v.tensor_scalar(out=rows2[:, :], in0=rows2[:, :], scalar1=EPS, scalar2=0.0, op0=ALU.add, op1=ALU.add), waits=[t])
        t = self.op("act", lambda a: a.activation(out=rows3[:, :], in_=rows2[:, :], func=AF.Sqrt), waits=[t])
        t = self.op("vector", lambda v: v.reciprocal(out=rows3[:, :], in_=rows3[:, :]), waits=[t])
        t = self.op("vector", lambda v: v.tensor_tensor(out=rows4, in0=rows3[:, :], in1=rows3[:, :], op=ALU.mult), waits=[t])
        t = self.op("vector", lambda v: v.tensor_tensor(out=rows4, in0=rows4, in1=rows2[:, :], op=ALU.mult), waits=[t])
        t = self.op("vector", lambda v: v.tensor_scalar(out=rows4, in0=rows4, scalar1=-0.5, scalar2=1.5, op0=ALU.mult, op1=ALU.add), waits=[t])
        t_rstd = self.op("vector", lambda v: v.tensor_tensor(out=rows3[:, :], in0=rows3[:, :], in1=rows4[:, :], op=ALU.mult), waits=[t])

        xn = env["xn"]
        norm_done = {}
        tmpf_user = {0: self.tok["tmpf_free"], 1: self.tok["tmpf_free"]}
        prev_half = None
        for h in (0, 1):
            def bc(t_, h=h):
                t_.matmul(pC[:, 0, :], env["ones_row"][:, :], rows[:, h * 512:(h + 1) * 512], start=True, stop=True)
                return t_.matmul(pC[:, 1, :], env["ones_row"][:, :], rows3[:, h * 512:(h + 1) * 512], start=True, stop=True)
            w = [t_rstd, self.tok["pC_free"]]
            if prev_half is not None:
                w.append(prev_half)
            t_bc = self.op("tensor", bc, waits=w)

            for dc in range(DC):
                bf = dc % 2
                t_ld = self.op("sync", lambda sy, dc=dc, bf=bf, h=h: sy.dma_start(
                    xch[:, bf, h * 512:(h + 1) * 512], src[dc, :, h * 512:(h + 1) * 512]),
                    waits=[xch_user[bf], self.tok["x_src_chunks"].get((dc, h), V0)],
                    by=16, sem="sXa" if bf == 0 else "sXb")
                t_n1 = self.op("vector", lambda v, bf=bf, h=h: v.tensor_tensor(
                    out=env["tmpf"][:, bf, :], in0=xch[:, bf, h * 512:(h + 1) * 512], in1=pC[:, 0, :], op=ALU.subtract),
                    waits=[t_ld, t_bc, tmpf_user[bf]])
                t_n2 = self.op("vector", lambda v, dc=dc, bf=bf, h=h: v.tensor_tensor(
                    out=xn[:, dc, h * 512:(h + 1) * 512], in0=env["tmpf"][:, bf, :], in1=pC[:, 1, :], op=ALU.mult),
                    waits=[t_n1, self.tok["xn_free"] if (h == 0 and dc == 0) else T0])
                norm_done[(h, dc)] = t_n2
                xch_user[bf] = t_n1
                tmpf_user[bf] = t_n2
            prev_half = norm_done[(h, DC - 1)]
        self.tok["pC_free"] = norm_done[(1, DC - 1)]
        self.tok["tmpf_free"] = norm_done[(1, DC - 1)]
        self.tok["x2bf_free"] = mm_done[DC - 1]
        return (norm_done[(0, DC - 1)], norm_done[(1, DC - 1)])

    def ffn_sublayer(self, src, dst, next_src=None):
        env = self.env
        xn = env["xn"]
        pA, pB, pC = env["pA"], env["pB"], env["pC"]

        xn_h = self.ln_pass(src)
        xn_ok = xn_h[1]

        def rmm(t, h):
            r = None
            for dc in range(DC):
                r = t.matmul(pB[0:NEXP, h, :], env["rw_sb"][:, dc, :],
                             xn[:, dc, h * 512:(h + 1) * 512], start=(dc == 0), stop=(dc == DC - 1))
            return r
        t_rmm = self.op("tensor", lambda t: rmm(t, 0),
                        waits=[xn_h[0], self.tok["small_w"], self.tok["pB_free"]])
        t_rmm = self.op("tensor", lambda t: rmm(t, 1), waits=[xn_h[1]])

        def revac(v):
            v.tensor_scalar(out=env["logit_sb"][:, 0:512], in0=pB[0:NEXP, 0, :],
                            scalar1=env["rbias_sb"][:, :], scalar2=1.0, op0=ALU.add, op1=ALU.mult)
            return v.tensor_scalar(out=env["logit_sb"][:, 512:1024], in0=pB[0:NEXP, 1, :],
                                   scalar1=env["rbias_sb"][:, :], scalar2=1.0, op0=ALU.add, op1=ALU.mult)
        t_lg = self.op("vector", revac, waits=[t_rmm, self.tok["pro_v"], self.tok["oneh"]])
        self.tok["pB_free"] = t_lg

        def ltr(t):
            r = None
            for rt in range(NROW):
                r = t.transpose(pC[:, 0, rt * 64:rt * 64 + NEXP],
                                env["logit_sb"][0:NEXP, rt * 128:(rt + 1) * 128],
                                env["ident_sb"][0:NEXP, 0:NEXP])
            return r
        t_ltr = self.op("tensor", ltr, waits=[t_lg, self.tok["pC_free"], self.tok["consts"]])

        probs, racc, spw = env["probs"], env["racc"], env["spw"]
        rmax, rsum = env["rmax"], env["rsum"]

        def smax1(v):
            r = None
            for rt in range(NROW):
                r = v.tensor_scalar(out=probs[:, rt, :], in0=pC[:, 0, rt * 64:rt * 64 + NEXP],
                                    scalar1=1.0, scalar2=-3.0e38, op0=ALU.mult, op1=ALU.max,
                                    accum_out=rmax[:, rt:rt + 1])
            return r
        t = self.op("vector", smax1, waits=[t_ltr])
        self.tok["pC_free"] = t
        t = self.op("vector", lambda v: v.tensor_scalar(
            out=rmax[:, :], in0=rmax[:, :], scalar1=-1.0, scalar2=0.0, op0=ALU.mult, op1=ALU.add), waits=[t])

        def smax3(a):
            r = None
            for rt in range(NROW):
                r = a.activation(out=probs[:, rt, :], in_=probs[:, rt, :], func=AF.Exp,
                                 bias=rmax[:, rt:rt + 1], accum_out=rsum[:, rt:rt + 1])
            return r
        t = self.op("act", smax3, waits=[t])
        t = self.op("vector", lambda v: v.reciprocal(out=rsum[:, :], in_=rsum[:, :]), waits=[t])

        def smax5(v):
            for rt in range(NROW):
                v.tensor_scalar(out=probs[:, rt, :], in0=probs[:, rt, :],
                                scalar1=rsum[:, rt:rt + 1], scalar2=0.0, op0=ALU.mult, op1=ALU.add)
            v.memset(racc[:, :, :], 0.0)
            return v.memset(env["racc2"][:, :, :], 0.0)
        t = self.op("vector", smax5, waits=[t])

        ta = tb = t
        for m in range(NEXP):
            acc = racc if m % 2 == 0 else env["racc2"]

            def cnt(v, m=m, acc=acc):
                r = None
                for rt in range(NROW):
                    r = v.scalar_tensor_tensor(out=acc[:, rt, :], in0=probs[:, rt, :],
                                               scalar=probs[:, rt, m:m + 1], in1=acc[:, rt, :],
                                               op0=ALU.is_lt, op1=ALU.add)
                return r
            if m % 2 == 0:
                ta = self.op("vector", cnt, waits=[ta])
            else:
                tb = self.op("vector", cnt, waits=[tb])

        def cmerge(v):
            r = None
            for rt in range(NROW):
                r = v.tensor_tensor(out=racc[:, rt, :], in0=racc[:, rt, :],
                                    in1=env["racc2"][:, rt, :], op=ALU.add)
            return r
        t = self.op("vector", cmerge, waits=[ta, tb])

        def selw(v):
            r = None
            for rt in range(NROW):
                r = v.scalar_tensor_tensor(out=spw[:, rt, :], in0=racc[:, rt, :],
                                           scalar=8.0, in1=probs[:, rt, :],
                                           op0=ALU.is_lt, op1=ALU.mult)
            return r
        t_spw = self.op("vector", selw, waits=[t])

        def str_(t):
            r = None
            for rt in range(NROW):
                r = t.transpose(pC[0:NEXP, rt // 4, (rt % 4) * 128:(rt % 4 + 1) * 128],
                                spw[:, rt, :], env["ident_sb"][:, :])
            return r
        t_str = self.op("tensor", str_, waits=[t_spw, self.tok["pC_free"]])

        def spwt(v):
            v.tensor_copy(env["spw_t"][:, 0:512], pC[0:NEXP, 0, :])
            return v.tensor_copy(env["spw_t"][:, 512:1024], pC[0:NEXP, 1, :])
        t_spwt = self.op("vector", spwt, waits=[t_str])
        self.tok["pC_free"] = t_spwt

        def wmm(t):
            t.matmul(pB[0:E, 0, :], env["m_onehot"][:, :], env["spw_t"][:, 0:512], start=True, stop=True)
            return t.matmul(pB[0:E, 1, :], env["m_onehot"][:, :], env["spw_t"][:, 512:1024], start=True, stop=True)
        t_wmm = self.op("tensor", wmm, waits=[t_spwt, self.tok["pB_free"]])

        def wevac(v):
            v.tensor_copy(env["wt_sb"][:, 0:512], pB[0:E, 0, :])
            return v.tensor_copy(env["wt_sb"][:, 512:1024], pB[0:E, 1, :])
        t_wt = self.op("vector", wevac, waits=[t_wmm])
        self.tok["pB_free"] = t_wt

        def wrepmm(t):
            t.matmul(pB[:, 0, :], env["e16_sb"][:, :], env["wt_sb"][:, 0:512], start=True, stop=True)
            return t.matmul(pB[:, 1, :], env["e16_sb"][:, :], env["wt_sb"][:, 512:1024], start=True, stop=True)
        t_wrep = self.op("tensor", wrepmm, waits=[t_wt])

        def wrepev(v):
            v.tensor_copy(env["wrep_sb"][:, 0:512], pB[:, 0, :])
            return v.tensor_copy(env["wrep_sb"][:, 512:1024], pB[:, 1, :])
        t_wrepev = self.op("vector", wrepev, waits=[t_wrep])
        self.tok["pB_free"] = t_wrepev

        def hmm(t):
            r = None
            for h in (0, 1):
                for dc in range(DC):
                    r = t.matmul(pC[:, h, :], env["a_sb"][:, dc, :], xn[:, dc, h * 512:(h + 1) * 512],
                                 start=(dc == 0), stop=(dc == DC - 1))
            return r
        t_hmm = self.op("tensor", hmm, waits=[xn_ok, self.tok["pC_free"]])
        del xn_ok

        def hwmul(v):
            v.tensor_tensor(out=env["hw_sb"][:, 0:512], in0=pC[:, 0, :], in1=env["wrep_sb"][:, 0:512], op=ALU.mult)
            return v.tensor_tensor(out=env["hw_sb"][:, 512:1024], in0=pC[:, 1, :], in1=env["wrep_sb"][:, 512:1024], op=ALU.mult)
        t_hw = self.op("vector", hwmul, waits=[t_hmm, t_wrepev])
        self.tok["pC_free"] = t_hw

        wslab, x1b, tmpf, resch = env["wslab"], env["x1buf"], env["tmpf"], env["resch"]
        slab_mm = {}
        evd = {}
        std = {}
        for oc in range(DC):
            wb = oc % 2
            t_w = self.op("gpsimd", lambda g, oc=oc, wb=wb: g.dma_start(
                wslab[:, wb, :, :], env["bw_s"][oc, :, :, :]),
                waits=[slab_mm.get(oc - 2, self.tok["wslab_free"])], by=16,
                sem="sWa" if wb == 0 else "sWb")

            for h in (0, 1):
                ob = h

                def mms(t, oc=oc, h=h, wb=wb, ob=ob):
                    for dc in range(DC):
                        t.matmul(pA[:, ob, :], wslab[:, wb, dc, :], xn[:, dc, h * 512:(h + 1) * 512],
                                 start=(dc == 0), stop=False)
                    return t.matmul(pA[:, ob, :], env["b_sb"][:, oc, :], env["hw_sb"][:, h * 512:(h + 1) * 512],
                                    start=False, stop=True)
                w = [t_w, evd.get((oc - 1, h), self.tok["pA_free"])]
                if oc == 0:
                    w += [xn_h[h], t_hw, self.tok["small_w"]]
                t_mm = self.op("tensor", mms, waits=w)
                slab_mm[oc] = t_mm

                t_r = self.op("sync", lambda sy, oc=oc, h=h, ob=ob: sy.dma_start(
                    resch[:, ob, :], src[oc, :, h * 512:(h + 1) * 512]),
                    waits=[evd.get((oc - 1, h), self.tok["resch_free"]),
                           self.tok["x_src_chunks"].get((oc, h), V0)],
                    by=16, sem="sRa" if ob == 0 else "sRb")

                t_e1 = self.op("vector", lambda v, oc=oc, ob=ob: v.tensor_scalar(
                    out=tmpf[:, ob, :], in0=pA[:, ob, :],
                    scalar1=0.5, scalar2=env["halfb"][:, oc:oc + 1], op0=ALU.mult, op1=ALU.add),
                    waits=[t_mm, self.tok["pro_v"],
                           evd.get((oc - 1, h), self.tok["tmpf_free"])])
                t_e2 = self.op("vector", lambda v, ob=ob: v.tensor_tensor(
                    out=x1b[:, ob, :], in0=tmpf[:, ob, :], in1=resch[:, ob, :], op=ALU.add),
                    waits=[t_e1, t_r, std.get((oc - 1, h), self.tok["x1b_free"][h])])
                evd[(oc, h)] = t_e2

                t_s = self.op("act", lambda sc, oc=oc, h=h, ob=ob: sc.dma_start(
                    dst[oc, :, h * 512:(h + 1) * 512], x1b[:, ob, :]),
                    waits=[t_e2], by=16, sem="sOa" if ob == 0 else "sOb")
                std[(oc, h)] = t_s
            if next_src is not None:
                if oc >= 4:
                    self.emit_stats_mm(oc - 4)
                if oc >= 2:
                    self.emit_stats_load(next_src, oc - 2, std)
        if next_src is not None:
            for dcx in (DC - 2, DC - 1):
                self.emit_stats_mm(dcx - 2)
                self.emit_stats_load(next_src, dcx, std)
            for dcx in (DC - 2, DC - 1):
                self.emit_stats_mm(dcx)

        self.tok["pA_free"] = evd[(DC - 1, 1)]
        self.tok["xn_free"] = slab_mm[DC - 1]
        self.tok["wslab_free"] = slab_mm[DC - 1]
        self.tok["resch_free"] = evd[(DC - 1, 1)]
        self.tok["tmpf_free"] = evd[(DC - 1, 1)]
        self.tok["x1b_free"] = {0: std[(DC - 1, 0)], 1: std[(DC - 1, 1)]}
        self.tok["last_store"] = [std[(DC - 1, 0)], std[(DC - 1, 1)]]
        self.tok["x_src"] = [std[(DC - 1, 0)], std[(DC - 1, 1)]]
        self.tok["x_src_chunks"] = dict(std)

    def attn_sublayer(self, dst, next_src=None):
        env = self.env
        xn = env["xn"]
        pA, pB, pC = env["pA"], env["pB"], env["pC"]
        wslab = env["wslab"]
        x_res = env["x_res"]

        xn_h = self.ln_pass(x_res)

        kv_store = {}
        slab_mm = {}
        evp = {}
        vtst_free = O0
        vtr_done = {}
        for idx in range(2 * NKV):
            isv = idx >= NKV
            g = idx % NKV
            wb = idx % 2
            wsrc = env["wv_s"] if isv else env["wk_s"]
            t_w = self.op("gpsimd", lambda gp, g=g, wb=wb, wsrc=wsrc: gp.dma_start(
                wslab[:, wb, :, :], wsrc[g, :, :, :]),
                waits=[slab_mm.get(idx - 2, self.tok["wslab_free"])], by=16,
                sem="sWa" if wb == 0 else "sWb")

            mmt = None
            for h in (0, 1):
                ob = h

                def mms(t, h=h, wb=wb, ob=ob):
                    r = None
                    for dc in range(DC):
                        r = t.matmul(pA[:, ob, :], wslab[:, wb, dc, :], xn[:, dc, h * 512:(h + 1) * 512],
                                     start=(dc == 0), stop=(dc == DC - 1))
                    return r
                w = [t_w, evp.get((idx - 1, h), self.tok["pA_free"])]
                if idx == 0:
                    w += [xn_h[h]]
                mmt = self.op("tensor", mms, waits=w)

                t_e = self.op("vector", lambda v, h=h, ob=ob, isv=isv, wb=wb: v.tensor_copy(
                    (env["vstage"] if isv else env["kstage"])[:, h * 512:(h + 1) * 512], pA[:, ob, :]),
                    waits=[mmt, evp.get((idx - 1, h), V0),
                           kv_store.get((idx - 1, h), O0),
                           vtr_done.get(idx - 1, V0),
                           self.tok.get("k2_%d" % (g - 1), V0) if not isv else V0])
                evp[(idx, h)] = t_e
            slab_mm[idx] = mmt

            if not isv:
                t_s = self.op("act", lambda sc, g=g: sc.dma_start(
                    env["kv_cont"][g, :, :], env["kstage"][:, :]),
                    waits=[evp[(idx, 0)], evp[(idx, 1)]], by=16, sem="sOK")
                kv_store[(idx, 0)] = kv_store[(idx, 1)] = t_s
                def flush_kn(g2):
                    def knmm(t):
                        t.matmul(pB[0:1, 0, :], env["ones_bf"][:, :], env["nsq"][:, 0:512], start=True, stop=True)
                        return t.matmul(pB[0:1, 1, :], env["ones_bf"][:, :], env["nsq"][:, 512:1024], start=True, stop=True)
                    t_kn = self.op("tensor", knmm,
                                   waits=[self.tok["k2_%d" % g2],
                                          self.tok.get("knred_%d" % (g2 - 1),
                                                       self.tok["pB_free"] if g2 == 0 else T0)])
                    self.tok["knmm_%d" % g2] = t_kn

                    def knred(v):
                        v.tensor_reduce(out=env["scr2"][0:1, 8:9], in_=pB[0:1, 0, :], axis=AX.X, op=ALU.max)
                        return v.tensor_reduce(out=env["scr2"][0:1, 9:10], in_=pB[0:1, 1, :], axis=AX.X, op=ALU.max)
                    t_kr = self.op("vector", knred, waits=[t_kn, self.tok.get("knred_%d" % (g2 - 1), V0)])
                    t_kr2 = self.op("vector", lambda v: v.tensor_reduce(
                        out=env["scr2"][0:1, 10:11], in_=env["scr2"][0:1, 8:10], axis=AX.X, op=ALU.max),
                        waits=[t_kr])
                    t_kr3 = self.op("vector", lambda v: v.tensor_tensor(
                        out=env["kn2max"][:, :], in0=env["kn2max"][:, :], in1=env["scr2"][0:1, 10:11], op=ALU.max),
                        waits=[t_kr2, self.tok["norminit"]])
                    self.tok["knred_%d" % g2] = t_kr3
                    return t_kr3

                if g > 0:
                    flush_kn(g - 1)
                t_k2 = self.op("vector", lambda v: v.tensor_tensor(
                    out=env["nsq"], in0=env["kstage"][:, :], in1=env["kstage"][:, :], op=ALU.mult),
                    waits=[evp[(idx, 0)], evp[(idx, 1)], self.tok.get("knmm_%d" % (g - 1), T0)])
                self.tok["k2_%d" % g] = t_k2
                if g == NKV - 1:
                    t_kr3 = flush_kn(g)
                    t_z = self.op("vector", lambda v: v.memset(env["vtstage"][:, :, :], 0.0),
                                  waits=[t_kr3, vtst_free])
                    t_knc = self.op("vector", lambda v: v.tensor_copy(
                        env["vtstage"][0:1, 0, 0:1], env["kn2max"][:, :]), waits=[t_z])
                    t_kns = self.op("act", lambda sc: sc.dma_start(
                        env["kv_cont"][16, :, :], env["vtstage"][:, :, :].rearrange("p a b -> p (a b)")),
                        waits=[t_knc, t_s], by=16, sem="sOK")
                    kv_store[(idx, 0)] = kv_store[(idx, 1)] = t_kns
                    self.tok["pB_free"] = t_kr3
                    self.tok["vt_zeroed"] = t_kns
            else:
                prev_tt = {}
                last_ev = None
                for tt in range(NROW):
                    def vtr1(t, tt=tt):
                        return t.transpose(env["pV"][:, tt % 2, 0:128], env["vstage"][:, tt * 128:(tt + 1) * 128],
                                           env["identb_sb"][:, :])
                    w = [evp[(idx, 0)], evp[(idx, 1)], self.tok["consts"]]
                    if tt < 2:
                        w.append(self.tok.get("pV_free", V0))
                    else:
                        w.append(prev_tt[tt - 2])
                    t_tr = self.op("tensor", vtr1, waits=w)
                    t_ev = self.op("vector", lambda v, tt=tt: v.tensor_copy(
                        env["vtstage"][:, tt, :], env["pV"][:, tt % 2, 0:128]),
                        waits=[t_tr, vtst_free, self.tok.get("vt_zeroed", O0)])
                    prev_tt[tt] = t_ev
                    last_ev = t_ev
                vtr_done[idx] = last_ev
                self.tok["pV_free"] = last_ev

                def vst(sc, s, g=g):
                    for tt in range(NROW):
                        sc.dma_start(env["kv_cont"][NKV + tt, :, g * 128:(g + 1) * 128],
                                     env["vtstage"][:, tt, :]).then_inc(s, 16)
                t_s = self.multi("act", vst, waits=[last_ev], n=NROW, sem="sOV")
                vtst_free = t_s
                kv_store[(idx, 0)] = kv_store[(idx, 1)] = t_s
        kv_all_stored = kv_store[(2 * NKV - 1, 0)]
        self.tok["pA_free"] = evp[(2 * NKV - 1, 1)]

        t_cc = self.op("gpsimd", lambda g: g.collective_compute(
            "AllGather", ALU.bypass,
            replica_groups=[[0, 1, 2, 3], [4, 5, 6, 7]],
            ins=[env["kv_cont"][:, :, :]],
            outs=[env["kv_gath"][:, :, :, :]]),
            waits=[kv_all_stored], sem="sC")

        owner = {}
        for rr in range(4):
            for ss, gg in enumerate(rows_for_rank(rr)):
                owner[gg] = (rr, ss)

        def reorder(g, s):
            for j in range(32):
                rr, ss = owner[j]
                g.dma_start(env["k_seq"][:, :, j * 128:(j + 1) * 128],
                            env["kv_gath"][rr, 0:NKV, :, ss * 128:(ss + 1) * 128]).then_inc(s, 16)
                g.dma_start(env["v_seq"][j, :, :],
                            env["kv_gath"][rr, NKV + ss, :, :]).then_inc(s, 16)
        t_reord = self.multi("sync", reorder, waits=[t_cc], n=64, sem="sRE")

        qs_store = {}
        qevd = {}
        q2_done = {}
        qn_mm = {}
        qn_red = {}
        for qh in range(NH):
            idx = 2 * NKV + qh
            wb = idx % 2
            t_w = self.op("gpsimd", lambda gp, qh=qh, wb=wb: gp.dma_start(
                wslab[:, wb, :, :], env["wq_s"][qh, :, :, :]),
                waits=[slab_mm.get(idx - 2, self.tok["wslab_free"])], by=16,
                sem="sWa" if wb == 0 else "sWb")

            mmt = None
            for h in (0, 1):
                ob = h

                def mms(t, h=h, wb=wb, ob=ob):
                    r = None
                    for dc in range(DC):
                        r = t.matmul(pA[:, ob, :], wslab[:, wb, dc, :], xn[:, dc, h * 512:(h + 1) * 512],
                                     start=(dc == 0), stop=(dc == DC - 1))
                    return r
                w = [t_w, qevd.get((qh - 1, h), evp.get((2 * NKV - 1, h), V0))]
                if qh == 0:
                    w.append(self.tok["pA_free"])
                mmt = self.op("tensor", mms, waits=w)
                t_e = self.op("vector", lambda v, h=h, ob=ob, qh=qh: v.tensor_copy(
                    env["qstage"][:, qh % 2, h * 512:(h + 1) * 512], pA[:, ob, :]),
                    waits=[mmt, qs_store.get(qh - 2, O0), q2_done.get(qh - 2, V0)])
                qevd[(qh, h)] = t_e
            slab_mm[idx] = mmt

            t_s = self.op("act", lambda sc, qh=qh: sc.dma_start(
                env["q_dram"][qh, :, :], env["qstage"][:, qh % 2, :]),
                waits=[qevd[(qh, 0)], qevd[(qh, 1)]], by=16,
                sem="sQa" if qh % 2 == 0 else "sQb")
            qs_store[qh] = t_s

            def flush_qn(qh2):
                def qnmm(t):
                    t.matmul(pB[0:1, 0, :], env["ones_bf"][:, :], env["hw_sb"][:, 0:512], start=True, stop=True)
                    return t.matmul(pB[0:1, 1, :], env["ones_bf"][:, :], env["hw_sb"][:, 512:1024], start=True, stop=True)
                t_qn = self.op("tensor", qnmm, waits=[q2_done[qh2], qn_red.get(qh2 - 1, self.tok["pB_free"])])
                qn_mm[qh2] = t_qn

                def qred(v):
                    v.tensor_reduce(out=env["scr2"][0:1, 0:1], in_=pB[0:1, 0, :], axis=AX.X, op=ALU.max)
                    return v.tensor_reduce(out=env["scr2"][0:1, 1:2], in_=pB[0:1, 1, :], axis=AX.X, op=ALU.max)
                t_qr = self.op("vector", qred, waits=[t_qn, self.tok["norminit"], qn_red.get(qh2 - 1, V0)])
                t_qr2 = self.op("vector", lambda v: v.tensor_reduce(
                    out=env["scr2"][0:1, 2:3], in_=env["scr2"][0:1, 0:2], axis=AX.X, op=ALU.max), waits=[t_qr])
                t_qr3 = self.op("vector", lambda v: v.tensor_tensor(
                    out=env["qn2max"][:, :], in0=env["qn2max"][:, :], in1=env["scr2"][0:1, 2:3], op=ALU.max), waits=[t_qr2])
                qn_red[qh2] = t_qr3

            if qh > 0:
                flush_qn(qh - 1)
            t_q2 = self.op("vector", lambda v, qh=qh: v.tensor_tensor(
                out=env["nsq"], in0=env["qstage"][:, qh % 2, :], in1=env["qstage"][:, qh % 2, :], op=ALU.mult),
                waits=[qevd[(qh, 0)], qevd[(qh, 1)], qn_mm.get(qh - 1, self.tok["xn_free"])])
            q2_done[qh] = t_q2
            if qh == NH - 1:
                flush_qn(qh)
        self.tok["pB_free"] = qn_red[NH - 1]
        self.tok["pA_free"] = qevd[(NH - 1, 1)]
        self.tok["xn_free"] = slab_mm[2 * NKV + NH - 1]
        self.tok["wslab_free"] = slab_mm[2 * NKV + NH - 1]
        qn_done = qn_red[NH - 1]
        qproj_mm_last = slab_mm[2 * NKV + NH - 1]

        t_knl = self.op("sync", lambda sy: sy.dma_start(
            env["knstage"][0:1, 0:4],
            env["kv_gath"][0:4, 16:17, 0:1, 0:1].rearrange("r a p t -> (a p) (r t)"),
            allow_slow_non_contiguous=True),
            waits=[t_cc, ("sOK", self.cnt.get("sOK", 0))], by=16, sem="sKN")
        t_kn1 = self.op("vector", lambda v: v.tensor_copy(env["scr2"][0:1, 11:15], env["knstage"][0:1, 0:4]),
                        waits=[t_knl])
        kn_done = self.op("vector", lambda v: v.tensor_reduce(
            out=env["kn2max"][:, :], in_=env["scr2"][0:1, 11:15], axis=AX.X, op=ALU.max),
            waits=[t_kn1])

        t = self.op("vector", lambda v: v.tensor_tensor(
            out=env["scr2"][0:1, 4:5], in0=env["qn2max"][:, :], in1=env["kn2max"][:, :], op=ALU.mult),
            waits=[qn_done, kn_done])
        t = self.op("act", lambda a: a.activation(out=env["scr2"][0:1, 5:6], in_=env["scr2"][0:1, 4:5], func=AF.Sqrt), waits=[t])
        t = self.op("vector", lambda v: v.tensor_scalar(
            out=env["scr2"][0:1, 6:7], in0=env["scr2"][0:1, 5:6], scalar1=-SCALE, scalar2=0.0, op0=ALU.mult, op1=ALU.add), waits=[t])
        t_nm = self.op("tensor", lambda te: te.matmul(
            pC[:, 0, 0:1], env["ones_row"][:, :], env["scr2"][0:1, 6:7], start=True, stop=True),
            waits=[t, self.tok["pC_free"]])
        t_nmc = self.op("vector", lambda v: v.tensor_copy(env["negm_sb"][:, :], pC[:, 0, 0:1]), waits=[t_nm])
        self.tok["pC_free"] = t_nmc

        wslab_h = env["wslab"]
        pt2_h = env["pt2"]

        def ptv(kb):
            if kb < 16:
                return bass.AP(wslab_h, kb * 512, [[8192, 128], [1, 512]])
            return pt2_h[:, kb - 16, :]

        def ptv2(j):
            if 2 * j < 16:
                return bass.AP(wslab_h, 2 * j * 512, [[8192, 128], [1, 1024]])
            return bass.AP(pt2_h, (2 * j - 16) * 512, [[8192, 128], [1, 1024]])

        def stb(kb):
            return (pA if (kb % 4) < 2 else pB)[:, kb % 2, :]

        mask_used = V0
        sl_free = {("k", 0): T0, ("k", 1): T0, ("v", 0): T0, ("v", 1): T0,
                   ("q", 0): T0, ("q", 1): T0}
        gidx = 0
        bank_free = {0: self.tok["pA_free"], 1: self.tok["pA_free"],
                     2: kn_done, 3: kn_done}
        pc_reader = {0: t_nmc, 1: self.tok["pC_free"]}
        last_exp = A0
        attn_done = None
        pend_lav = []
        pend_fin = None
        pop_last = T0

        def flush_fin():
            nonlocal pend_fin, attn_done
            if pend_fin is None:
                return
            ctx = pend_fin
            pend_fin = None
            cur = ctx["cur"]
            def _recip(v, pl=ctx["pl"]):
                with self.nc.allow_low_precision(reason="1/l in bf16: 0.4% rel on the softmax scale, within tolerance"):
                    return v.reciprocal(out=env["linv_bf"], in_=pC[0:1, pl, :])
            t_linv = self.op("vector", _recip,
                             waits=[cur["t_l"], self.tok.get("lb_last", T0)])
            t_lb = self.op("tensor", lambda te, pl=ctx["pl"]: te.matmul(
                pC[:, pl, :], env["ones_rbf"][:, :], env["linv_bf"], start=True, stop=True),
                waits=[t_linv])
            self.tok["lb_last"] = t_lb
            t_lbc = self.op("vector", lambda v, pl=ctx["pl"]: v.tensor_copy(env["linvb_sb"], pC[:, pl, :]),
                            waits=[t_lb])
            pc_reader[ctx["pl"]] = t_lbc

            def evac(v, g=ctx["g"], i=ctx["i"], pav=ctx["pav"]):
                return v.tensor_tensor(out=xn[:, 4 * g:4 * g + 4, i * 128:(i + 1) * 128],
                                       in0=pC[:, pav, :], in1=env["linvb_sb"], op=ALU.mult)
            t_ev = self.op("vector", evac, waits=[cur["t_av"], t_lbc, qproj_mm_last])
            pc_reader[ctx["pav"]] = t_ev
            sl_free[("v", 0)] = cur["t_av"]
            attn_done = t_ev

        for i in range(NROW):
            NB = 4 * (i + 1)
            L = 512 * (i + 1)
            mw = [mask_used]
            if i == 0:
                mw += [q2_done[NH - 1], qs_store[NH - 1], qs_store[NH - 2]]
            t_mask = self.op("sync", lambda sy, i=i: sy.dma_start(
                env["masks_sb"], env["masks"][i, :, :, :]),
                waits=mw, by=16, sem="sMK")
            mask_last = None

            for g in range(NKV):
                pav = g % 2
                pl = 1 - pav
                par = gidx % 2
                gidx += 1
                t_kld = self.op("sync", lambda sy, g=g, L=L, par=par: sy.dma_start(
                    env["ksl"][:, par, 0:L], env["k_seq"][g, :, 0:L]),
                    waits=[sl_free[("k", par)], t_reord], by=16,
                    sem="sKL" if par == 0 else "sKLb")
                t_qld = self.op("sync", lambda sy, g=g, i=i, par=par: sy.dma_start(
                    bass.AP(env["qsl"], par * 512, [[1024, 128], [128, 4], [1, 128]]),
                    env["q_dram"][4 * g:4 * g + 4, :, i * 128:(i + 1) * 128].rearrange("h p t -> p h t")),
                    waits=[qs_store[4 * g + 3], qs_store[4 * g + 2], sl_free[("q", par)]],
                    by=16, sem="sQL" if par == 0 else "sQLb")

                pt_ready = {}
                t_st = None
                cur = {"t_l": None, "t_av": None, "t_vld": None}

                def emit_vld(g=g, NB=NB, cur=cur):
                    cur["t_vld"] = self.op("sync", lambda sy, g=g, NB=NB: sy.dma_start(
                        env["vsl"][:, 0:NB, :],
                        env["v_seq"][0:NB, :, g * 128:(g + 1) * 128].rearrange("b p v -> p b v")),
                        waits=[sl_free[("v", 0)], t_reord], by=16, sem="sVL")

                def emit_l(kb, NB=NB, pl=pl, pt_ready=pt_ready, cur=cur):
                    lw = [pt_ready[kb]]
                    if kb == 0:
                        lw.append(pc_reader[pl])
                    cur["t_l"] = self.op("tensor", lambda t, kb=kb, NB=NB, pl=pl: t.matmul(
                        pC[0:1, pl, :], env["ones_bf"][:, :], ptv(kb),
                        start=(kb == 0), stop=(kb == NB - 1), skip_group_check=True), waits=lw)

                def emit_av(kb, NB=NB, pav=pav, pt_ready=pt_ready, cur=cur):
                    aw = [pt_ready[kb]]
                    if kb == 0:
                        aw += [cur["t_vld"], pc_reader[pav]]
                    cur["t_av"] = self.op("tensor", lambda t, kb=kb, NB=NB, pav=pav: t.matmul(
                        pC[:, pav, :], env["vsl"][:, kb, :], ptv(kb),
                        start=(kb == 0), stop=(kb == NB - 1), skip_group_check=True), waits=aw)


                if not pend_lav:
                    flush_fin()
                    emit_vld()
                for kb in range(NB):
                    def stmm(t, kb=kb, par=par):
                        return t.matmul(stb(kb), env["ksl"][:, par, kb * 128:(kb + 1) * 128],
                                        bass.AP(env["qsl"], par * 512, [[1024, 128], [128, 4], [1, 128]]), start=True, stop=True,
                                        skip_group_check=True)
                    t_st = self.op("tensor", stmm, waits=[t_kld, t_qld, bank_free[kb % 4]])
                    if pend_lav:
                        el, ea, k1 = pend_lav.pop(0)
                        if pend_lav:
                            _, _, k2 = pend_lav.pop(0)
                            el(k1)
                            el(k2)
                            ea(k1)
                            ea(k2)
                        else:
                            el(k1)
                            ea(k1)
                        pop_last = pend_fin["cur"]["t_av"]
                        if not pend_lav:
                            flush_fin()
                            emit_vld()

                    if kb % 2 == 1:
                        j = kb // 2

                        def expf(a, j=j):
                            return a.activation(out=ptv2(j), in_=(pA if j % 2 == 0 else pB)[:, :, :],
                                                func=AF.Exp, bias=env["negm_sb"][:, :], scale=SCALE)
                        t_ex = self.op("act", expf, waits=[t_st, t_nmc, pop_last])
                        last_exp = t_ex
                        b0 = (2 * j) % 4
                        bank_free[b0] = t_ex
                        bank_free[b0 + 1] = t_ex
                        for kk in (2 * j, 2 * j + 1):
                            pt_ready[kk] = t_ex
                            if kk >= NB - 4:
                                jj = kk - (NB - 4)
                                t_mk = self.op("vector", lambda v, kk=kk, jj=jj: v.tensor_tensor(
                                    out=ptv(kk), in0=ptv(kk),
                                    in1=bass.AP(env["qstage"], jj * 512, [[2048, 128], [1, 512]]),
                                    op=ALU.mult),
                                    waits=[t_ex, t_mask])
                                mask_last = t_mk
                                pt_ready[kk] = t_mk
                        if j >= 4 and (j - 4) % 2 == 1:
                            k0 = 2 * (j - 5)
                            for kk in range(k0, k0 + 4):
                                emit_l(kk)
                            for kk in range(k0, k0 + 4):
                                emit_av(kk)
                for kbx in range(max(0, NB - 8), NB):
                    pend_lav.append((emit_l, emit_av, kbx))
                pend_fin = {"g": g, "i": i, "pav": pav, "pl": pl, "cur": cur}

                sl_free[("k", par)] = t_st
                sl_free[("q", par)] = t_st
            mask_used = mask_last
        while pend_lav:
            el, ea, k1 = pend_lav.pop(0)
            el(k1)
            ea(k1)
        flush_fin()

        self.tok["pB_free"] = last_exp
        self.tok["pC_free"] = attn_done
        self.tok["wslab_free"] = attn_done

        evd = {}
        std = {}
        slab_mm2 = {}
        for oc in range(DC):
            wb = oc % 2
            t_w = self.op("gpsimd", lambda g, oc=oc, wb=wb: g.dma_start(
                wslab[:, wb, :, :], env["wo_s"][oc, :, :, :]),
                waits=[slab_mm2.get(oc - 2, self.tok["wslab_free"])], by=16,
                sem="sWa" if wb == 0 else "sWb")
            for h in (0, 1):
                ob = h

                def mms(t, oc=oc, h=h, wb=wb, ob=ob):
                    r = None
                    for dc in range(DC):
                        r = t.matmul(pA[:, ob, :], wslab[:, wb, dc, :], xn[:, dc, h * 512:(h + 1) * 512],
                                     start=(dc == 0), stop=(dc == DC - 1))
                    return r
                w = [t_w, evd.get((oc - 1, h), V0)]
                if oc == 0:
                    w.append(attn_done)
                t_mm = self.op("tensor", mms, waits=w)
                slab_mm2[oc] = t_mm

                t_r = self.op("sync", lambda sy, oc=oc, h=h, ob=ob: sy.dma_start(
                    env["resch"][:, ob, :], x_res[oc, :, h * 512:(h + 1) * 512]),
                    waits=[evd.get((oc - 1, h), self.tok["resch_free"]),
                           self.tok["x_src_chunks"].get((oc, h), V0)],
                    by=16, sem="sRa" if ob == 0 else "sRb")

                t_e2 = self.op("vector", lambda v, ob=ob: v.tensor_tensor(
                    out=env["x1buf"][:, ob, :], in0=pA[:, ob, :], in1=env["resch"][:, ob, :], op=ALU.add),
                    waits=[t_mm, t_r, std.get((oc - 1, h), self.tok["x1b_free"][h]),
                           evd.get((oc - 1, h), V0)])
                evd[(oc, h)] = t_e2

                t_s = self.op("act", lambda sc, oc=oc, h=h, ob=ob: sc.dma_start(
                    dst[oc, :, h * 512:(h + 1) * 512], env["x1buf"][:, ob, :]),
                    waits=[t_e2], by=16, sem="sOa" if ob == 0 else "sOb")
                std[(oc, h)] = t_s
            if next_src is not None:
                if oc >= 4:
                    self.emit_stats_mm(oc - 4)
                if oc >= 2:
                    self.emit_stats_load(next_src, oc - 2, std)
        if next_src is not None:
            for dcx in (DC - 2, DC - 1):
                self.emit_stats_mm(dcx - 2)
                self.emit_stats_load(next_src, dcx, std)
            for dcx in (DC - 2, DC - 1):
                self.emit_stats_mm(dcx)

        self.tok["pA_free"] = evd[(DC - 1, 1)]
        self.tok["xn_free"] = slab_mm2[DC - 1]
        self.tok["wslab_free"] = slab_mm2[DC - 1]
        self.tok["resch_free"] = evd[(DC - 1, 1)]
        self.tok["x1b_free"] = {0: std[(DC - 1, 0)], 1: std[(DC - 1, 1)]}
        self.tok["last_store"] = [std[(DC - 1, 0)], std[(DC - 1, 1)]]
        self.tok["x_src"] = [std[(DC - 1, 0)], std[(DC - 1, 1)]]
        self.tok["x_src_chunks"] = dict(std)

    def emit(self):
        q = self.q
        blk = self.env["blk"]

        @blk.sync
        def _(eng):
            for f in q["sync"]:
                f(eng)

        @blk.scalar
        def _(eng):
            for f in q["act"]:
                f(eng)

        @blk.gpsimd
        def _(eng):
            for f in q["gpsimd"]:
                f(eng)

        @blk.tensor
        def _(eng):
            for f in q["tensor"]:
                f(eng)

        @blk.vector
        def _(eng):
            for f in q["vector"]:
                f(eng)


def build_program():
    nc = bass.Bass("TRN2", target_bir_lowering=False)
    env = {}

    def din(name, shape, dt=F32):
        env[name] = nc.dram_tensor(name, shape, dt, kind="ExternalInput").ap()

    din("x_in", [DC, 128, TOK])
    din("wq_s", [NH, 128, DC, 128])
    din("wk_s", [NKV, 128, DC, 128])
    din("wv_s", [NKV, 128, DC, 128])
    din("wo_s", [DC, 128, DC, 128])
    din("bw_s", [DC, 128, DC, 128])
    din("bb_s", [128, DC])
    din("rw_s", [128, DC, NEXP])
    din("rb_s", [NEXP, 1])
    din("a_s", [128, DC, 128])
    din("b2_s", [128, DC, 128])
    din("emap", [NEXP, 1], I32)
    din("iota8", [NEXP, E])
    din("e16", [E, 128])
    din("ident32", [128, 128])
    din("identb", [128, 128], BF16)
    din("masks", [NROW, 128, 4, 512], BF16)

    env["out_x"] = nc.dram_tensor("out_x", [DC, 128, TOK], F32, kind="ExternalOutput").ap()
    env["x_res"] = nc.dram_tensor("x_res", [DC, 128, TOK], F32).ap()
    env["q_dram"] = nc.dram_tensor("q_dram", [NH, 128, TOK], BF16).ap()
    env["kv_cont"] = nc.dram_tensor("kv_cont", [17, 128, TOK], BF16).ap()
    env["kv_gath"] = nc.dram_tensor("kv_gath", [4, 17, 128, TOK], BF16).ap()
    env["k_seq"] = nc.dram_tensor("k_seq", [NKV, 128, S], BF16).ap()
    env["v_seq"] = nc.dram_tensor("v_seq", [32, 128, NKV * 128], BF16).ap()

    sb = [
        ("xn", [128, DC, TOK], BF16),
        ("wslab", [128, 2, DC, 128], BF16),
        ("xchunk", [128, 2, TOK], F32),
        ("xbf", [128, 2, TOK], BF16),
        ("x2bf", [128, 2, TOK], BF16),
        ("tmpf", [128, 2, 512], F32),
        ("resch", [128, 2, 512], F32),
        ("x1buf", [128, 2, 512], F32),
        ("ones_bf", [128, 1], BF16),
        ("ones_row", [1, 128], F32),
        ("ones_rbf", [1, 128], BF16),
        ("ident_sb", [128, 128], F32),
        ("identb_sb", [128, 128], BF16),
        ("rows", [1, 1024], F32),
        ("rows2", [1, 1024], F32),
        ("rows3", [1, 1024], F32),
        ("halfb", [128, DC], F32),
        ("rbias_sb", [NEXP, 1], F32),
        ("emap_i", [NEXP, 1], I32),
        ("emap_f", [NEXP, 1], F32),
        ("iota_sb", [NEXP, E], F32),
        ("m_onehot", [NEXP, E], F32),
        ("e16_sb", [E, 128], F32),
        ("rw_sb", [128, DC, NEXP], BF16),
        ("a_sb", [128, DC, 128], BF16),
        ("b_sb", [128, DC, 128], BF16),
        ("probs", [128, NROW, NEXP], F32),
        ("racc", [128, NROW, NEXP], F32),
        ("racc2", [128, NROW, NEXP], F32),
        ("spw", [128, NROW, NEXP], F32),
        ("rmax", [128, NROW], F32),
        ("rsum", [128, NROW], F32),
        ("logit_sb", [NEXP, TOK], F32),
        ("spw_t", [NEXP, TOK], F32),
        ("wrep_sb", [128, TOK], F32),
        ("hw_sb", [128, TOK], BF16),
        ("kstage", [128, TOK], BF16),
        ("vstage", [128, TOK], BF16),
        ("vtstage", [128, NROW, 128], BF16),
        ("qstage", [128, 2, TOK], BF16),
        ("qn2max", [1, 1], F32),
        ("kn2max", [1, 1], F32),
        ("scr2", [1, 16], F32),
        ("negm_sb", [128, 1], F32),
        ("knstage", [1, 4], BF16),
        ("ksl", [128, 2, S], BF16),
        ("vsl", [128, 32, 128], BF16),
        ("qsl", [128, 2, 512], BF16),
        ("pt2", [128, 16, 512], BF16),
    ]
    ps = [("pA", [128, 2, 512], F32), ("pB", [128, 2, 512], F32), ("pC", [128, 2, 512], F32), ("pV", [128, 2, 1024], BF16)]
    sems = ["sT", "sV", "sA", "sC", "sXa", "sXb", "sRa", "sRb", "sMK", "sKL", "sKLb", "sVL", "sQL", "sQLb", "sKN", "sCN", "sOa", "sOb", "sOK", "sOV", "sQa", "sQb", "sWa", "sWb", "sRE", "sSW"]

    import contextlib
    with contextlib.ExitStack() as ctx:
        for name, shape, dt in sb:
            env[name] = ctx.enter_context(nc.sbuf_tensor(name, shape, dt))
        env["rows4"] = env["logit_sb"][0:1, :]
        env["linv_bf"] = bass.AP(env["vtstage"], 0, [[1024, 1], [1, 512]])
        env["wt_sb"] = env["spw_t"][0:E, :]
        env["masks_sb"] = bass.AP(env["qstage"], 0, [[2048, 128], [512, 4], [1, 512]])
        env["nsq"] = env["hw_sb"][:, :]
        env["linv_row"] = env["spw_t"][0:1, 0:512]
        env["linvb_sb"] = env["wrep_sb"][:, 0:512]
        for name, shape, pdt in ps:
            env[name] = ctx.enter_context(nc.psum_tensor(name, shape, pdt))
        for name in sems:
            env[name] = ctx.enter_context(nc.semaphore(name))
        env["blk"] = ctx.enter_context(nc.Block())
        p = P(nc, env)
        p.plan()
        p.emit()
    return nc



def _slab(w):
    din, dout = w.shape
    return np.ascontiguousarray(
        np.asarray(w, dtype=np.float32).reshape(din // 128, 128, dout // 128, 128).transpose(2, 1, 0, 3))


def _host_shared(wq, wk, wv, wo, base_W, base_b, router_W, router_b, A, Bm, emap):
    import ml_dtypes
    d = {}
    wq = np.asarray(wq, dtype=np.float32).reshape(D, NH, HD)
    d["wq_s"] = np.ascontiguousarray(
        wq.reshape(DC, 128, NH, HD).transpose(2, 1, 0, 3))
    wk = np.asarray(wk, dtype=np.float32).reshape(D, NKV, HD)
    d["wk_s"] = np.ascontiguousarray(wk.reshape(DC, 128, NKV, HD).transpose(2, 1, 0, 3))
    wv = np.asarray(wv, dtype=np.float32).reshape(D, NKV, HD)
    d["wv_s"] = np.ascontiguousarray(wv.reshape(DC, 128, NKV, HD).transpose(2, 1, 0, 3))
    d["wo_s"] = _slab(wo)
    bw = np.asarray(base_W, dtype=np.float32)
    d["bw_s"] = np.ascontiguousarray(bw.reshape(DC, 128, DC, 128).transpose(0, 3, 2, 1))
    d["bb_s"] = np.ascontiguousarray(np.asarray(base_b, dtype=np.float32).reshape(DC, 128).T)
    rw = np.asarray(router_W, dtype=np.float32)
    d["rw_s"] = np.ascontiguousarray(rw.T.reshape(DC, 128, NEXP).transpose(1, 0, 2))
    d["rb_s"] = np.ascontiguousarray(np.asarray(router_b, dtype=np.float32).reshape(NEXP, 1))
    Aa = np.asarray(A, dtype=np.float32)
    d["a_s"] = np.ascontiguousarray(Aa.reshape(E * R, D).T.reshape(DC, 128, E * R).transpose(1, 0, 2))
    Bb = np.asarray(Bm, dtype=np.float32)
    d["b2_s"] = np.ascontiguousarray(
        Bb.transpose(0, 2, 1).reshape(E * R, DC, 128))
    d["emap"] = np.ascontiguousarray(np.asarray(emap, dtype=np.int32).reshape(NEXP, 1))
    d["iota8"] = np.ascontiguousarray(np.tile(np.arange(E, dtype=np.float32), (NEXP, 1)))
    e16 = np.zeros((E, 128), dtype=np.float32)
    for e in range(E):
        e16[e, e * R:(e + 1) * R] = 2.0
    d["e16"] = e16
    d["ident32"] = np.eye(128, dtype=np.float32)
    d["identb"] = np.eye(128, dtype=ml_dtypes.bfloat16)
    return d


def _mask_for_rank(r):
    import ml_dtypes
    rows = rows_for_rank(r)
    m = np.zeros((NROW, 4, 128, 512), dtype=np.float32)
    for i, g in enumerate(rows):
        qpos = g * 128 + np.arange(128)
        for j in range(4):
            kpos = (4 * i + j) * 128 + np.arange(128)
            vis = (kpos[:, None] <= qpos[None, :]).astype(np.float32)
            m[i, j] = np.tile(vis, (1, 4))
    m = m.transpose(0, 2, 1, 3)
    return np.ascontiguousarray(m.astype(ml_dtypes.bfloat16))


_NC_CACHE = {}


def kernel(L0_attn_x, L0_attn_wq, L0_attn_wk, L0_attn_wv, L0_attn_wo,
           L0_ffn_base_W, L0_ffn_base_b, L0_ffn_router_W, L0_ffn_router_b,
           L0_ffn_A, L0_ffn_B, L0_ffn_expert_map):
    x = np.asarray(L0_attn_x, dtype=np.float32)
    shared = _host_shared(L0_attn_wq, L0_attn_wk, L0_attn_wv, L0_attn_wo,
                          L0_ffn_base_W, L0_ffn_base_b, L0_ffn_router_W,
                          L0_ffn_router_b, L0_ffn_A, L0_ffn_B, L0_ffn_expert_map)
    if "nc" not in _NC_CACHE:
        _NC_CACHE["nc"] = build_program()
    nc = _NC_CACHE["nc"]

    in_maps = []
    meta = []
    for c in range(8):
        b, r = c // 4, c % 4
        rows = rows_for_rank(r)
        meta.append((b, rows))
        xi = np.concatenate([x[b, g * 128:(g + 1) * 128, :] for g in rows], axis=0)
        xi = np.ascontiguousarray(xi.T.reshape(DC, 128, TOK))
        im = dict(shared)
        im["x_in"] = xi
        im["masks"] = _mask_for_rank(r)
        in_maps.append(im)

    res = run_bass_kernel_spmd(nc, in_maps, core_ids=list(range(8)))

    out = np.empty((B, S, D), dtype=np.float32)
    for c in range(8):
        b, rows = meta[c]
        oc = np.asarray(res.results[c]["out_x"])
        ocv = np.ascontiguousarray(oc.reshape(D, TOK).T)
        for i, g in enumerate(rows):
            out[b, g * 128:(g + 1) * 128, :] = ocv[i * 128:(i + 1) * 128, :]
    return out
